# version 25
# baseline (speedup 1.0000x reference)
import contextlib
import math

import numpy as np
import ml_dtypes

import concourse.bass as bass
import concourse.mybir as mybir
from concourse.bass_utils import run_bass_kernel_spmd

F32 = mybir.dt.float32
BF16 = mybir.dt.bfloat16
AF = mybir.ActivationFunctionType
ALU = mybir.AluOpType
AX = mybir.AxisListType
BF = ml_dtypes.bfloat16

PI = math.pi
RSQRT_DH = 1.0 / math.sqrt(128.0)
EPS = 1e-5

ENGS = ("sync", "gpsimd", "tensor", "scalar", "vector")


class Sem:
    def __init__(self, ctx, nc, name):
        self.h = ctx.enter_context(nc.semaphore(name))
        self.n = 0
        self.name = name


def build(phases="ABCD", debug=False):
    nc = bass.Bass("TRN2", target_bir_lowering=False, debug=False)

    xt = nc.dram_tensor("xt", [2048, 2048], F32, kind="ExternalInput").ap()
    wqt = nc.dram_tensor("wqt", [16, 128, 2048], F32, kind="ExternalInput").ap()
    wkt = nc.dram_tensor("wkt", [16, 128, 2048], F32, kind="ExternalInput").ap()
    wvt = nc.dram_tensor("wvt", [16, 128, 2048], F32, kind="ExternalInput").ap()
    wot = nc.dram_tensor("wot", [16, 128, 2048], F32, kind="ExternalInput").ap()
    p1t = nc.dram_tensor("p1t", [64, 128, 2048], F32, kind="ExternalInput").ap()
    p2t = nc.dram_tensor("p2t", [16, 128, 8192], F32, kind="ExternalInput").ap()
    bq2 = nc.dram_tensor("bq2", [128, 16], F32, kind="ExternalInput").ap()
    bk2 = nc.dram_tensor("bk2", [128, 16], F32, kind="ExternalInput").ap()
    bv2 = nc.dram_tensor("bv2", [128, 16], F32, kind="ExternalInput").ap()
    bo2 = nc.dram_tensor("bo2", [128, 16], F32, kind="ExternalInput").ap()
    b22 = nc.dram_tensor("b22", [128, 16], F32, kind="ExternalInput").ap()
    b1b = nc.dram_tensor("b1b", [128, 64], F32, kind="ExternalInput").ap()
    helix2 = nc.dram_tensor("helix2", [128, 16], F32, kind="ExternalInput").ap()
    spiralr = nc.dram_tensor("spiralr", [1, 16], F32, kind="ExternalInput").ap()
    ones_sq = nc.dram_tensor("ones_sq", [128, 128], BF16, kind="ExternalInput").ap()
    ones1f = nc.dram_tensor("ones1f", [1, 128], F32, kind="ExternalInput").ap()

    def scratch(name, shape, dt):
        kind = "ExternalOutput" if debug else "Internal"
        return nc.dram_tensor(name, shape, dt, kind=kind).ap()

    k_scr = scratch("k_scr", [2048, 2048], BF16)
    v_scr = scratch("v_scr", [16, 2048, 128], BF16)
    x1_scr = scratch("x1_scr", [2048, 1024], F32)
    outT = nc.dram_tensor("outT", [2048, 1024], F32, kind="ExternalOutput").ap()
    if debug:
        dbg_q = nc.dram_tensor("dbg_q", [16, 128, 1024], BF16,
                               kind="ExternalOutput").ap()
        dbg_at = nc.dram_tensor("dbg_at", [16, 128, 1024], BF16,
                                kind="ExternalOutput").ap()
        dbg_nb = nc.dram_tensor("dbg_nb", [128, 1], F32,
                                kind="ExternalOutput").ap()

    with contextlib.ExitStack() as ctx:
        def sbuf(name, shape, dt, **kw):
            return ctx.enter_context(nc.sbuf_tensor(name, shape, dt, **kw))

        pool16 = [sbuf(f"pool16_{i}", [128, 2048], BF16, is_transpose_dest=True)
                  for i in range(16)]
        pool_q = [sbuf(f"poolq_{i}", [128, 1024], BF16) for i in range(16)]
        pool_a = [sbuf(f"poola_{i}", [128, 1024], BF16) for i in range(16)]
        wslab = [sbuf(f"wslab_{i}", [128, 2048], BF16) for i in range(6)]
        stage = [sbuf(f"stage_{i}", [128, 2048], BF16) for i in range(2)]
        stage2 = [sbuf(f"stage2_{i}", [128, 2048], BF16) for i in range(2)]
        pt_sb = [sbuf(f"pt_{i}", [128, 1024], BF16) for i in range(2)]
        tmp_f = sbuf("tmp_f", [128, 1024], F32)
        gp_tmp = sbuf("gp_tmp", [128, 1024], F32)
        mbc_sb = sbuf("mbc_sb", [128, 1024], F32)
        rbc_sb = sbuf("rbc_sb", [128, 1024], F32)
        rowsc = sbuf("rowsc", [1, 4096], F32)
        helix_sb = sbuf("helix_sb", [128, 16], F32)
        t_a = sbuf("t_a", [128, 16], F32)
        t_b = sbuf("t_b", [128, 16], F32)
        w_sb = sbuf("w_sb", [128, 16], F32)
        cw_sb = sbuf("cw_sb", [128, 16], F32)
        sinv = sbuf("sinv", [128, 16], F32)
        cosv = sbuf("cosv", [128, 16], F32)
        qsc = sbuf("qsc", [128, 16], F32)
        bqs = sbuf("bqs", [128, 16], F32)
        bks = sbuf("bks", [128, 16], F32)
        bq2_sb = sbuf("bq2_sb", [128, 16], F32)
        bk2_sb = sbuf("bk2_sb", [128, 16], F32)
        bv2_sb = sbuf("bv2_sb", [128, 16], F32)
        bo2_sb = sbuf("bo2_sb", [128, 16], F32)
        b22_sb = sbuf("b22_sb", [128, 16], F32)
        b1_sb = sbuf("b1_sb", [128, 64], F32)
        spiral_bc = sbuf("spiral_bc", [128, 16], F32)
        spiralr_sb = sbuf("spiralr_sb", [1, 16], F32)
        ones_sq_sb = sbuf("ones_sq_sb", [128, 128], BF16)
        ones1f_sb = sbuf("ones1f_sb", [1, 128], F32)
        qkn = sbuf("qkn", [1, 128], F32)
        negB = sbuf("negB", [128, 1], F32)

        psA = ctx.enter_context(nc.psum_tensor("psA", [128, 2048], F32))
        psB2 = ctx.enter_context(nc.psum_tensor("psB2", [128, 1024], F32))
        psC = ctx.enter_context(nc.psum_tensor("psC", [128, 1024], F32))

        r_sb = [pool16[i][:].bitcast(F32) for i in range(16)]
        r2_sb = [pool_a[i][:].bitcast(F32) for i in range(16)]
        x1stage = [stage2[i][:].bitcast(F32) for i in range(2)]
        recip = stage2[0][:].bitcast(F32)
        denraw_f = stage2[0][:].bitcast(F32)
        avraw_f = stage2[1][:].bitcast(F32)
        xres_v = [stage[0][:].bitcast(F32), stage[1][:].bitcast(F32)]
        x1r = [stage[0][:].bitcast(F32)[:, 0:512],
               stage[0][:].bitcast(F32)[:, 512:1024]]
        outstage = [stage[1][:].bitcast(F32)[:, 0:512],
                    stage[1][:].bitcast(F32)[:, 512:1024]]
        rbf = pt_sb[0]
        rsq = pt_sb[1]
        rbf2 = pt_sb[0][:, 0:512]
        rsq2 = pt_sb[1][:, 0:512]

        def S(name):
            return Sem(ctx, nc, name)

        s_bias = S("s_bias")
        s_xt_t = [S(f"s_xt{i}") for i in range(16)]
        s_w = [S(f"s_w{i}") for i in range(6)]
        s_st = [S("s_stA"), S("s_stB")]
        s_vt = [S(f"s_vt{i}") for i in range(16)]
        s_kh = [S("s_khA"), S("s_khB")]
        s_xr = [S("s_xrA"), S("s_xrB")]
        s_x1w = [S("s_x1wA"), S("s_x1wB")]
        s_x1r = [S("s_x1rA"), S("s_x1rB")]
        s_out = [S("s_outA"), S("s_outB")]
        s_pe = S("s_pe")
        s_gp = S("s_gp")
        s_act = S("s_act")
        s_dve = S("s_dve")
        s_dbg = S("s_dbg")

        streams = {k: [] for k in ENGS}
        last_wait = {k: {} for k in ENGS}
        defer_state = {"on": False, "bufs": None}

        def em(eng, fn):
            if defer_state["on"]:
                defer_state["bufs"][eng].append(fn)
            else:
                streams[eng].append(fn)

        def begin_defer():
            defer_state["on"] = True
            defer_state["bufs"] = {k: [] for k in ENGS}

        def end_defer():
            defer_state["on"] = False
            bufs = defer_state["bufs"]
            defer_state["bufs"] = None
            return bufs

        def splice(bufs):
            for k in ENGS:
                streams[k].extend(bufs[k])

        def wait(eng, sem, val):
            if val is None or val <= 0:
                return
            if defer_state["on"]:
                em(eng, lambda e, s=sem.h, v=val: e.wait_ge(s, v))
                return
            lw = last_wait[eng]
            if lw.get(sem.name, -1) >= val:
                return
            lw[sem.name] = val
            em(eng, lambda e, s=sem.h, v=val: e.wait_ge(s, v))

        def dma(eng, out_ap, in_ap, sem, **kw):
            sem.n += 16
            def f(e, o=out_ap, i=in_ap, s=sem.h, kw=kw):
                e.dma_start(out=o, in_=i, **kw).then_inc(s, 16)
            em(eng, f)
            return sem.n

        def dma_t(eng, out_ap, in_ap, sem):
            sem.n += 16
            def f(e, o=out_ap, i=in_ap, s=sem.h):
                e.dma_start_transpose(o, i).then_inc(s, 16)
            em(eng, f)
            return sem.n

        def mm(out_ap, l_ap, r_ap, start, stop, inc=None):
            if inc is not None:
                inc.n += 1
            def f(t, o=out_ap, l=l_ap, r=r_ap, st=start, sp=stop,
                  s=(inc.h if inc else None)):
                ins = t.matmul(o, l, r, start=st, stop=sp)
                if s is not None:
                    ins.then_inc(s, 1)
            em("tensor", f)
            return inc.n if inc else None

        def act(out_ap, in_ap, func, scale=1.0, bias=0.0, accum=None, inc=None):
            if inc is not None:
                inc.n += 1
            def f(a, o=out_ap, i=in_ap, fu=func, sc=scale, bi=bias,
                  au=accum, s=(inc.h if inc else None)):
                ins = a.activation(out=o, in_=i, func=fu, scale=sc, bias=bi,
                                   accum_out=au)
                if s is not None:
                    ins.then_inc(s, 1)
            em("scalar", f)
            return inc.n if inc else None

        def vec(name, *args, inc=None, **kw):
            if inc is not None:
                inc.n += 1
            def f(v, n=name, a=args, kw=kw, s=(inc.h if inc else None)):
                ins = getattr(v, n)(*a, **kw)
                if s is not None:
                    ins.then_inc(s, 1)
            em("vector", f)
            return inc.n if inc else None

        def vdrain():
            em("vector", lambda v: v.drain())

        def gvec(name, *args, inc=None, **kw):
            if inc is not None:
                inc.n += 1
            def f(g, n=name, a=args, kw=kw, s=(inc.h if inc else None)):
                ins = getattr(g, n)(*a, **kw)
                if s is not None:
                    ins.then_inc(s, 1)
            em("gpsimd", f)
            return inc.n if inc else None

        def adrain():
            em("scalar", lambda a: a.drain())

        wload_cnt = [0] * 6

        def load_slab(src_ap, gate_pe):
            b = sum(wload_cnt) % 6
            if gate_pe:
                wait("gpsimd", s_pe, gate_pe)
            dma("gpsimd", wslab[b][:], src_ap, s_w[b], max_dma_last_dim=8192)
            wload_cnt[b] += 1
            return b, 16 * wload_cnt[b]

        for dst, src in ((helix_sb, helix2), (bq2_sb, bq2), (bk2_sb, bk2),
                         (bv2_sb, bv2), (bo2_sb, bo2), (b22_sb, b22),
                         (b1_sb, b1b), (spiralr_sb, spiralr),
                         (ones_sq_sb, ones_sq), (ones1f_sb, ones1f)):
            dma("sync", dst[:], src[:], s_bias)
        BIAS_ALL = s_bias.n

        for e in range(16):
            dma("gpsimd", pool16[e][:], xt[e * 128:(e + 1) * 128, :],
                s_xt_t[e], max_dma_last_dim=8192)

        wait("tensor", s_bias, BIAS_ALL)
        m_spiral = mm(psC[:, 0:16], ones1f_sb[:], spiralr_sb[:], True, True,
                      inc=s_pe)
        wait("vector", s_bias, BIAS_ALL)
        wait("vector", s_pe, m_spiral)
        vec("tensor_copy", spiral_bc[:], psC[:, 0:16])
        vdrain()
        vec("tensor_single_scalar", t_a[:], helix_sb[:], PI, ALU.is_gt)
        vdrain()
        vec("scalar_tensor_tensor", t_b[:], t_a[:], -2 * PI, helix_sb[:],
            ALU.mult, ALU.add)
        vdrain()
        vec("tensor_single_scalar", t_a[:], t_b[:], -PI, ALU.is_lt)
        vdrain()
        vec("scalar_tensor_tensor", w_sb[:], t_a[:], 2 * PI, t_b[:],
            ALU.mult, ALU.add)
        vdrain()
        vec("tensor_single_scalar", t_b[:], w_sb[:], PI / 2, ALU.add)
        vdrain()
        vec("tensor_single_scalar", t_a[:], t_b[:], PI, ALU.is_gt)
        vdrain()
        m_trig = vec("scalar_tensor_tensor", cw_sb[:], t_a[:], -2 * PI, t_b[:],
                     ALU.mult, ALU.add, inc=s_dve)
        wait("scalar", s_dve, m_trig)
        act(sinv[:], w_sb[:], AF.Sin)
        adrain()
        m_sin = act(cosv[:], cw_sb[:], AF.Sin, inc=s_act)
        wait("vector", s_act, m_sin)
        vec("scalar_tensor_tensor", qsc[:], cosv[:], RSQRT_DH, spiral_bc[:],
            ALU.mult, ALU.mult)
        vdrain()
        vec("tensor_mul", bqs[:], bq2_sb[:], qsc[:])
        vdrain()
        vec("memset", qkn[0:1, 120:121], EPS)
        vec("memset", qkn[0:1, 122:123], -0.5)
        vdrain()
        CONST_DONE = vec("tensor_mul", bks[:], bk2_sb[:], sinv[:], inc=s_dve)

        j_pe = {}
        q_evac_act = {}
        qn_dve = {}
        k_evac_act = {}
        kst_dma = {}
        kn_dve = {}
        v_evac_act = {}
        vst_dma = {}

        def proj_phase(tag, w_srcs, consumer, gates, init_gates):
            nw = min(6, len(w_srcs))
            pend = [load_slab(w_srcs[i], init_gates[i]) for i in range(nw)]
            for j in range(len(w_srcs)):
                b, wv = pend[j]
                wait("tensor", s_w[b], wv)
                consumer(j, b)
                nj = j + 6
                if nj < len(w_srcs):
                    pend.append(load_slab(w_srcs[nj], gates(j)))

        q_sq_act = {}

        def q_norm_mms(j):
            wait("tensor", s_act, q_sq_act[j])
            if j > 0:
                wait("tensor", s_dve, qn_dve[j - 1])
            for sc in range(2):
                mm(psC[0:1, sc * 512:(sc + 1) * 512], ones_sq_sb[:, 0:1],
                   pt_sb[0][:, sc * 512:(sc + 1) * 512], True, True,
                   inc=(s_pe if sc == 1 else None))
            j_pe[("qn", j)] = s_pe.n
            wait("vector", s_pe, j_pe[("qn", j)])
            qn_dve[j] = vec("tensor_reduce", qkn[0:1, j:j + 1],
                            psC[0:1, 0:1024], AX.X, ALU.max, inc=s_dve)

        def q_consumer(j, b):
            rq = j % 2
            if j >= 2:
                wait("tensor", s_act, q_evac_act[j - 2])
            for e in range(16):
                if j == 0:
                    wait("tensor", s_xt_t[e], 16)
                for sc in range(2):
                    mm(psA[:, rq * 1024 + sc * 512:rq * 1024 + (sc + 1) * 512],
                       wslab[b][:, e * 128:(e + 1) * 128],
                       pool16[e][:, sc * 512:(sc + 1) * 512],
                       start=(e == 0), stop=(e == 15),
                       inc=(s_pe if (e == 15 and sc == 1) else None))
            j_pe[("q", j)] = s_pe.n
            if j > 0:
                q_norm_mms(j - 1)
            wait("scalar", s_dve, CONST_DONE)
            wait("scalar", s_pe, j_pe[("q", j)])
            q_evac_act[j] = act(pool_q[j][:],
                                psA[:, rq * 1024:(rq + 1) * 1024],
                                AF.Identity, scale=qsc[:, j:j + 1],
                                bias=bqs[:, j:j + 1], inc=s_act)
            adrain()
            if j > 0:
                wait("scalar", s_pe, j_pe[("qn", j - 1)])
            q_sq_act[j] = act(pt_sb[0][:], pool_q[j][:], AF.Square, inc=s_act)

        proj_phase("q", [wqt[j] for j in range(16)], q_consumer,
                   lambda j: j_pe[("q", j)], [None] * 6)
        q_norm_mms(15)

        k_sq_act = {}
        kn_prev = [None]

        def k_norm_mms(jh):
            j, half = jh
            wait("tensor", s_act, k_sq_act[(j, half)])
            prev = (j, 0) if half == 1 else ((j - 1, 1) if j > 0 else None)
            if prev is not None:
                wait("tensor", s_dve, kn_dve[prev])
            else:
                wait("tensor", s_dve, qn_dve[15])
            for sc in range(2):
                mm(psC[0:1, sc * 512:(sc + 1) * 512], ones_sq_sb[:, 0:1],
                   stage2[0][:, half * 1024 + sc * 512:
                             half * 1024 + (sc + 1) * 512],
                   True, True, inc=(s_pe if sc == 1 else None))
            j_pe[("kn", j, half)] = s_pe.n
            wait("vector", s_pe, j_pe[("kn", j, half)])
            kn_dve[(j, half)] = vec(
                "tensor_reduce",
                qkn[0:1, 16 + 32 * half + j:17 + 32 * half + j],
                psC[0:1, 0:1024], AX.X, ALU.max, inc=s_dve)

        def k_consumer(j, b):
            p = j % 2
            for half in range(2):
                if j > 0:
                    wait("tensor", s_act, k_evac_act[(j - 1, half)])
                else:
                    wait("tensor", s_act, q_evac_act[14 + half])
                for e in range(16):
                    for sc in range(2):
                        mm(psA[:, half * 1024 + sc * 512:
                               half * 1024 + (sc + 1) * 512],
                           wslab[b][:, e * 128:(e + 1) * 128],
                           pool16[e][:, half * 1024 + sc * 512:
                                     half * 1024 + (sc + 1) * 512],
                           start=(e == 0), stop=(e == 15),
                           inc=(s_pe if (e == 15 and sc == 1) else None))
                j_pe[("k", j, half)] = s_pe.n
                if kn_prev[0] is not None:
                    k_norm_mms(kn_prev[0])
                wait("scalar", s_pe, j_pe[("k", j, half)])
                if half == 0 and j >= 2:
                    wait("scalar", s_st[p], kst_dma[j - 2])
                k_evac_act[(j, half)] = act(
                    stage[p][:, half * 1024:(half + 1) * 1024],
                    psA[:, half * 1024:(half + 1) * 1024], AF.Identity,
                    scale=sinv[:, j:j + 1], bias=bks[:, j:j + 1], inc=s_act)
                adrain()
                if kn_prev[0] is not None:
                    wait("scalar", s_pe, j_pe[("kn",) + kn_prev[0]])
                k_sq_act[(j, half)] = act(
                    stage2[0][:, half * 1024:(half + 1) * 1024],
                    stage[p][:, half * 1024:(half + 1) * 1024],
                    AF.Square, inc=s_act)
                kn_prev[0] = (j, half)
            wait("sync", s_act, k_evac_act[(j, 1)])
            kst_dma[j] = dma("sync", k_scr[j * 128:(j + 1) * 128, :],
                             stage[p][:], s_st[p])

        proj_phase("k", [wkt[j] for j in range(16)], k_consumer,
                   lambda j: j_pe[("k", j, 1)],
                   [j_pe[("q", 10 + i)] for i in range(6)])
        k_norm_mms(kn_prev[0])

        vdrain()
        vec("tensor_max", qkn[0:1, 80:96], qkn[0:1, 16:32], qkn[0:1, 48:64])
        vdrain()
        vec("tensor_mul", qkn[0:1, 96:112], qkn[0:1, 0:16], qkn[0:1, 80:96])
        vdrain()
        mB = vec("tensor_reduce", qkn[0:1, 112:113], qkn[0:1, 96:112],
                 AX.X, ALU.max, inc=s_dve)
        wait("scalar", s_dve, mB)
        mBs = act(qkn[0:1, 113:114], qkn[0:1, 112:113], AF.Sqrt, inc=s_act)
        wait("vector", s_act, mBs)
        mBn = vec("tensor_single_scalar", qkn[0:1, 114:115],
                  qkn[0:1, 113:114], -1.0, ALU.mult, inc=s_dve)
        wait("tensor", s_dve, mBn)
        mBb = mm(psC[:, 0:1], ones1f_sb[:], qkn[0:1, 114:115], True, True,
                 inc=s_pe)
        wait("scalar", s_pe, mBb)
        NEGB_ACT = act(negB[:], psC[:, 0:1], AF.Copy, inc=s_act)

        def v_consumer(j, b):
            p = j % 2
            for half in range(2):
                if j > 0:
                    wait("tensor", s_act, v_evac_act[(j - 1, half)])
                else:
                    wait("tensor", s_act, k_evac_act[(15, half)])
                if j == 0 and half == 0:
                    wait("tensor", s_act, NEGB_ACT)
                for e in range(16):
                    for sc in range(2):
                        mm(psA[:, half * 1024 + sc * 512:
                               half * 1024 + (sc + 1) * 512],
                           wslab[b][:, e * 128:(e + 1) * 128],
                           pool16[e][:, half * 1024 + sc * 512:
                                     half * 1024 + (sc + 1) * 512],
                           start=(e == 0), stop=(e == 15),
                           inc=(s_pe if (e == 15 and sc == 1) else None))
                j_pe[("v", j, half)] = s_pe.n
                wait("scalar", s_pe, j_pe[("v", j, half)])
                if half == 0:
                    if j >= 2:
                        wait("scalar", s_st[p], vst_dma[j - 2])
                    else:
                        wait("scalar", s_st[p], kst_dma[14 + j])
                v_evac_act[(j, half)] = act(
                    stage[p][:, half * 1024:(half + 1) * 1024],
                    psA[:, half * 1024:(half + 1) * 1024], AF.Identity,
                    bias=bv2_sb[:, j:j + 1], inc=s_act)
            wait("sync", s_act, v_evac_act[(j, 1)])
            vst_dma[j] = dma("sync",
                             v_scr[:, j * 128:(j + 1) * 128, :]
                             .rearrange("tc p c -> p tc c"),
                             stage[p][:].rearrange("p (tc c) -> p tc c",
                                                   c=128),
                             s_st[p])

        wait("sync", s_st[0], kst_dma[0])
        kh0_dma = dma("sync", stage2[1][:], k_scr[0:128, :], s_kh[0])
        proj_phase("v", [wvt[j] for j in range(16)], v_consumer,
                   lambda j: j_pe[("v", j, 1)],
                   [j_pe[("k", 10 + i, 1)] for i in range(6)])
        V_LAST_PE = j_pe[("v", 15, 1)]

        wait("sync", s_st[0], s_st[0].n)
        wait("sync", s_st[1], s_st[1].n)
        wait("sync", s_pe, V_LAST_PE)
        dma_t("sync", pool16[0][:], v_scr[0], s_vt[0])
        dma_t("sync", pool16[1][:], v_scr[1], s_vt[1])
        kh1_dma = dma("sync", stage[1][:], k_scr[128:256, :], s_kh[1])
        for tc in range(2, 16):
            dma_t("sync", pool16[tc][:], v_scr[tc], s_vt[tc])
        wait("scalar", s_st[0], s_st[0].n)
        wait("scalar", s_st[1], s_st[1].n)
        wait("scalar", s_pe, V_LAST_PE)
        act(qkn[0:1, 121:122], qkn[0:1, 120:121], AF.Exp)

        if debug:
            for j in range(16):
                wait("sync", s_act, q_evac_act[j])
                dma("sync", dbg_q[j], pool_q[j][:], s_dbg)
            wait("sync", s_act, NEGB_ACT)
            dma("sync", dbg_nb[:], negB[:], s_dbg)
        DBG_A = s_dbg.n

        if "B" in phases or "C" in phases or "D" in phases:
            kh_dma = {}
            exp_act = {}
            avcp_act = {}
            avcp_dve = {}
            avd_pe = {}
            sc_pe = {}
            norm_dve = {}
            kh_dma[0] = kh0_dma
            kh_dma[1] = kh1_dma
            def emit_davd(h, tc):
                pt = pt_buf(h, tc)
                wait("tensor", s_act, exp_act[(h, tc)])
                if h == 0:
                    wait("tensor", s_vt[tc], 16)
                if tc == 0 and h > 0:
                    wait("tensor", s_act, avcp_act[h - 1])
                    wait("tensor", s_dve, avcp_dve[h - 1])
                for sc in range(2):
                    mm(psC[:, sc * 512:(sc + 1) * 512], ones_sq_sb[:],
                       pt[:, sc * 512:(sc + 1) * 512],
                       start=(tc == 0), stop=(tc == 15))
                for sc in range(2):
                    mm(psB2[:, sc * 512:(sc + 1) * 512],
                       pool16[tc][:, h * 128:(h + 1) * 128],
                       pt[:, sc * 512:(sc + 1) * 512],
                       start=(tc == 0), stop=(tc == 15),
                       inc=(s_pe if sc == 1 else None))
                avd_pe[(h, tc)] = s_pe.n

            def pt_buf(h, tc):
                if h == 0:
                    return [pt_sb[0], pt_sb[1], pool_a[14], pool_a[15]][tc % 4]
                return pt_sb[tc % 2]

            for h in range(16):
                p = h % 2
                kh_buf = stage2[1] if h == 0 else stage[p]
                wait("tensor", s_kh[p], kh_dma[h])
                for tc in range(16):
                    r = tc % 2
                    if tc >= 2:
                        wait("tensor", s_act, exp_act[(h, tc - 2)])
                    elif h > 0:
                        wait("tensor", s_act, exp_act[(h - 1, 14 + tc)])
                    else:
                        wait("tensor", s_act, v_evac_act[(15, tc)])
                    for sc in range(2):
                        mm(psA[:, r * 1024 + sc * 512:
                               r * 1024 + (sc + 1) * 512],
                           kh_buf[:, tc * 128:(tc + 1) * 128],
                           pool_q[h][:, sc * 512:(sc + 1) * 512],
                           True, True, inc=(s_pe if sc == 1 else None))
                    sc_pe[(h, tc)] = s_pe.n
                    wait("scalar", s_act, NEGB_ACT)
                    wait("scalar", s_pe, sc_pe[(h, tc)])
                    if h == 0:
                        if tc >= 4:
                            wait("scalar", s_pe, avd_pe[(h, tc - 4)])
                    elif tc >= 2:
                        wait("scalar", s_pe, avd_pe[(h, tc - 2)])
                    else:
                        wait("scalar", s_pe, avd_pe[(h - 1, 14 + tc)])
                    exp_act[(h, tc)] = act(
                        pt_buf(h, tc)[:], psA[:, r * 1024:(r + 1) * 1024],
                        AF.Exp, bias=negB[:], inc=s_act)
                    if tc > 0:
                        emit_davd(h, tc - 1)
                emit_davd(h, 15)
                wait("scalar", s_pe, avd_pe[(h, 15)])
                if h > 0:
                    wait("scalar", s_dve, norm_dve[h - 1])
                avcp_act[h] = act(denraw_f[:, 0:1024], psC[:], AF.Copy,
                                  inc=s_act)
                wait("vector", s_pe, avd_pe[(h, 15)])
                avcp_dve[h] = vec("tensor_copy", avraw_f[:, 0:1024], psB2[:],
                                  inc=s_dve)
                vdrain()
                wait("vector", s_act, avcp_act[h])
                vec("reciprocal", tmp_f[:], denraw_f[:, 0:1024])
                vdrain()
                norm_dve[h] = vec("tensor_mul", pool_a[h][:],
                                  avraw_f[:, 0:1024], tmp_f[:], inc=s_dve)
                vdrain()
                nh = h + 2
                if nh < 16:
                    pn = nh % 2
                    wait("sync", s_pe, sc_pe[(h, 15)])
                    kh_dma[nh] = dma("sync", stage[pn][:],
                                     k_scr[nh * 128:(nh + 1) * 128, :],
                                     s_kh[pn])
            ATT_DONE_DVE = norm_dve[15]
            ATT_LAST_EXP = exp_act[(15, 15)]
            ATT_LAST_AVD = avd_pe[(15, 15)]
            ATT_LAST_SC = sc_pe[(15, 15)]

            if debug:
                for h in range(16):
                    wait("sync", s_dve, norm_dve[h])
                    dma("sync", dbg_at[h], pool_a[h][:], s_dbg)
            DBG_B = s_dbg.n

        if "C" in phases or "D" in phases:
            wo_pe = {}
            stt_dve = {}
            rbf_dve = {}
            sq_act = {}
            stats_pe = {}
            xr_dma = {}
            x1_dve = {}
            x1w_dma = {}

            wait("sync", s_pe, sc_pe[(14, 15)])
            xr_dma[0] = dma("sync", xres_v[0], xt[0:128, 0:1024], s_xr[0])
            wait("sync", s_pe, ATT_LAST_SC)
            xr_dma[1] = dma("sync", xres_v[1], xt[128:256, 0:1024], s_xr[1])

            def c_stats_mms(j):
                wait("tensor", s_dve, rbf_dve[j])
                wait("tensor", s_act, sq_act[j])
                for sc in range(2):
                    mm(psB2[0:1, sc * 512:(sc + 1) * 512], ones_sq_sb[:, 0:1],
                       rbf[:, sc * 512:(sc + 1) * 512],
                       start=(j == 0), stop=(j == 15))
                for sc in range(2):
                    mm(psC[0:1, sc * 512:(sc + 1) * 512], ones_sq_sb[:, 0:1],
                       rsq[:, sc * 512:(sc + 1) * 512],
                       start=(j == 0), stop=(j == 15),
                       inc=(s_pe if sc == 1 else None))
                stats_pe[j] = s_pe.n

            pend = [load_slab(wot[i], V_LAST_PE) for i in range(6)]
            for j in range(16):
                b, wv = pend[j]
                wait("tensor", s_w[b], wv)
                r = j % 2
                if j < 2:
                    wait("tensor", s_act, ATT_LAST_EXP)
                    wait("tensor", s_dve, ATT_DONE_DVE)
                else:
                    wait("tensor", s_dve, stt_dve[j - 2])
                for e in range(16):
                    for sc in range(2):
                        mm(psA[:, r * 1024 + sc * 512:
                               r * 1024 + (sc + 1) * 512],
                           wslab[b][:, e * 128:(e + 1) * 128],
                           pool_a[e][:, sc * 512:(sc + 1) * 512],
                           start=(e == 0), stop=(e == 15),
                           inc=(s_pe if (e == 15 and sc == 1) else None))
                wo_pe[j] = s_pe.n
                if j > 0:
                    c_stats_mms(j - 1)
                wait("vector", s_pe, wo_pe[j])
                wait("vector", s_xr[r], xr_dma[j])
                if j == 0:
                    wait("vector", s_pe, ATT_LAST_AVD)
                stt_dve[j] = vec("scalar_tensor_tensor", r_sb[j],
                                 psA[:, r * 1024:(r + 1) * 1024],
                                 bo2_sb[:, j:j + 1], xres_v[r],
                                 ALU.add, ALU.add, inc=s_dve)
                vdrain()
                if j > 0:
                    wait("vector", s_pe, stats_pe[j - 1])
                rbf_dve[j] = vec("tensor_copy", rbf[:], r_sb[j], inc=s_dve)
                wait("scalar", s_dve, stt_dve[j])
                if j > 0:
                    wait("scalar", s_pe, stats_pe[j - 1])
                sq_act[j] = act(rsq[:], r_sb[j], AF.Square, inc=s_act)
                nj = j + 2
                if nj < 16:
                    wait("sync", s_dve, stt_dve[j])
                    xr_dma[nj] = dma("sync", xres_v[r],
                                     xt[nj * 128:(nj + 1) * 128, 0:1024],
                                     s_xr[r])
                if j + 6 < 16:
                    pend.append(load_slab(wot[j + 6], wo_pe[j]))
            c_stats_mms(15)

            wait("vector", s_pe, stats_pe[15])
            vec("tensor_single_scalar", rowsc[0:1, 0:1024], psB2[0:1, 0:1024],
                1.0 / 2048.0, ALU.mult)
            vec("tensor_single_scalar", rowsc[0:1, 1024:2048],
                psC[0:1, 0:1024], 1.0 / 2048.0, ALU.mult)
            vdrain()
            vec("tensor_mul", rowsc[0:1, 2048:3072], rowsc[0:1, 0:1024],
                rowsc[0:1, 0:1024])
            vdrain()
            m_var = vec("tensor_sub", rowsc[0:1, 3072:4096],
                        rowsc[0:1, 1024:2048], rowsc[0:1, 2048:3072],
                        inc=s_dve)
            wait("scalar", s_dve, m_var)
            act(rowsc[0:1, 2048:3072], rowsc[0:1, 3072:4096], AF.Ln,
                bias=qkn[0:1, 120:121])
            adrain()
            m_rstd = act(rowsc[0:1, 1024:2048], rowsc[0:1, 2048:3072],
                         AF.Exp, scale=qkn[0:1, 122:123], inc=s_act)
            wait("tensor", s_act, m_rstd)
            for sc in range(2):
                mm(psB2[:, sc * 512:(sc + 1) * 512], ones1f_sb[:],
                   rowsc[0:1, sc * 512:(sc + 1) * 512], True, True)
            for sc in range(2):
                mm(psC[:, sc * 512:(sc + 1) * 512], ones1f_sb[:],
                   rowsc[0:1, 1024 + sc * 512:1024 + (sc + 1) * 512],
                   True, True, inc=(s_pe if sc == 1 else None))
            BC1_PE = s_pe.n

            wait("vector", s_pe, BC1_PE)
            vec("tensor_copy", mbc_sb[:], psB2[:])
            CP1 = vec("tensor_copy", rbc_sb[:], psC[:], inc=s_dve)
            vdrain()
            wait("vector", s_pe, ATT_LAST_SC)
            if debug:
                em("vector", lambda e, s=s_dbg: e.wait_ge(s.h, s.n))
            wait("gpsimd", s_dve, CP1)
            wait("gpsimd", s_pe, ATT_LAST_SC)
            if debug:
                em("gpsimd", lambda e, s=s_dbg: e.wait_ge(s.h, s.n))
            pend1_init = [load_slab(p1t[i], wo_pe[10 + i]) for i in range(6)]
            x1_done = {}
            pass3_done = {}
            h5_0 = slice(0, 512)
            for j0 in range(0, 16, 2):
                if j0 < 10:
                    eng, es, vf, tw = "vector", s_dve, vec, tmp_f
                else:
                    eng, es, vf, tw = "gpsimd", s_gp, gvec, gp_tmp
                for k in range(2):
                    vf("tensor_sub", tw[:, k * 512:(k + 1) * 512],
                       r_sb[j0 + k][:, h5_0], mbc_sb[:, h5_0], inc=es)
                if eng == "vector":
                    vdrain()
                else:
                    wait(eng, es, es.n)
                for k in range(2):
                    x1_done[(0, j0 + k)] = (es, vf(
                        "tensor_mul", pool_q[j0 + k][:, h5_0],
                        tw[:, k * 512:(k + 1) * 512], rbc_sb[:, h5_0],
                        inc=es))
                if eng == "vector":
                    vdrain()
                else:
                    wait(eng, es, es.n)
            X1_H0 = [x1_done[(0, 9)], x1_done[(0, 15)]]

            idx = 0
            for j in range(16):
                h5_1 = slice(512, 1024)
                vec("tensor_sub", tmp_f[:, 0:512], r_sb[j][:, h5_1],
                    mbc_sb[:, h5_1])
                vdrain()
                x1_done[(1, j)] = (s_dve, vec(
                    "tensor_mul", pool_q[j][:, h5_1], tmp_f[:, 0:512],
                    rbc_sb[:, h5_1], inc=s_dve))
                vdrain()
                for half in range(2):
                    h5 = slice(half * 512, (half + 1) * 512)
                    p = idx % 2
                    x1sl = x1stage[p][:, 0:512]
                    if idx >= 2:
                        wait("vector", s_x1w[p], x1w_dma[idx - 2])
                    vec("tensor_sub", tmp_f[:, 0:512], r_sb[j][:, h5],
                        mbc_sb[:, h5])
                    vdrain()
                    mv = vec("tensor_mul", x1sl, tmp_f[:, 0:512],
                             rbc_sb[:, h5], inc=s_dve)
                    vdrain()
                    wait("sync", s_dve, mv)
                    x1w_dma[idx] = dma(
                        "sync", x1_scr[j * 128:(j + 1) * 128, h5],
                        x1sl, s_x1w[p])
                    idx += 1
                pass3_done[j] = mv
            X1_ALL = [x1_done[(0, 15)], (s_dve, pass3_done[15])]

        if "D" in phases:
            f1_pe = {}
            gelu_act = {}
            qr_pe = {}
            f2_pe = {}
            stt2_dve = {}
            rbf2_dve = {}
            sq2_act = {}
            stats2_pe = {}
            x1r_dma = {}
            out_done = {}
            out_dma = {}
            CP2_prev = [0]
            ln2_deferred = [None]

            def h_view(hc, sc2):
                return pool16[hc // 4][:, (hc % 4) * 512:(hc % 4 + 1) * 512]

            ln2_rstd = {}

            def emit_ln2_fin(sc2):
                wait("vector", s_pe, stats2_pe[(sc2, 15)])
                vec("tensor_single_scalar", rowsc[0:1, 0:512],
                    psC[0:1, 0:512], 1.0 / 2048.0, ALU.mult)
                vec("tensor_single_scalar", rowsc[0:1, 512:1024],
                    psC[0:1, 512:1024], 1.0 / 2048.0, ALU.mult)
                vdrain()
                vec("tensor_mul", rowsc[0:1, 1024:1536], rowsc[0:1, 0:512],
                    rowsc[0:1, 0:512])
                vdrain()
                m_var2 = vec("tensor_sub", rowsc[0:1, 1536:2048],
                             rowsc[0:1, 512:1024], rowsc[0:1, 1024:1536],
                             inc=s_dve)
                wait("scalar", s_dve, m_var2)
                act(rowsc[0:1, 1024:1536], rowsc[0:1, 1536:2048], AF.Ln,
                    bias=qkn[0:1, 120:121])
                adrain()
                m_rstd2 = act(rowsc[0:1, 512:1024], rowsc[0:1, 1024:1536],
                              AF.Exp, scale=qkn[0:1, 122:123], inc=s_act)
                ln2_rstd[sc2] = m_rstd2
                wait("tensor", s_act, m_rstd2)
                mm(psC[:, 0:512], ones1f_sb[:], rowsc[0:1, 0:512], True, True)
                mm(psC[:, 512:1024], ones1f_sb[:], rowsc[0:1, 512:1024],
                   True, True, inc=s_pe)
                BC2_PE = s_pe.n
                wait("vector", s_pe, BC2_PE)
                if sc2 == 0:
                    wait("vector", s_gp, x1_done[(0, 15)][1])
                else:
                    wait("vector", out_done[(0, 14)][0],
                         out_done[(0, 14)][1])
                vec("tensor_copy", mbc_sb[:, 0:512], psC[:, 0:512])
                CP2 = vec("tensor_copy", rbc_sb[:, 0:512], psC[:, 512:1024],
                          inc=s_dve)
                vdrain()
                CP2_prev[0] = CP2
                if sc2 == 1:
                    wait("gpsimd", s_dve, CP2)

            def emit_ln2_out(sc2):

                    for eo in range(16):
                        p = eo % 2
                        if eo % 2 == 0 and sc2 == 1:
                            eng, es, ew = "gpsimd", s_gp, gp_tmp[:, 0:512]
                            vf = gvec
                        else:
                            eng, es, ew = "vector", s_dve, tmp_f[:, 0:512]
                            vf = vec
                        wait(eng, s_dve, CP2_prev[0])
                        if eo >= 2:
                            wait(eng, s_out[p], out_dma[(sc2, eo - 2)])
                        elif sc2 == 1:
                            wait(eng, s_out[p], out_dma[(0, 14 + eo)])
                        vf("tensor_sub", ew, r2_sb[eo], mbc_sb[:, 0:512], inc=es)
                        if eng == "vector":
                            vdrain()
                        else:
                            wait(eng, es, es.n)
                        out_done[(sc2, eo)] = (es, vf(
                            "tensor_mul", outstage[p], ew, rbc_sb[:, 0:512],
                            inc=es))
                        if eng == "vector":
                            vdrain()
                        wait("sync", es, out_done[(sc2, eo)][1])
                        out_dma[(sc2, eo)] = dma(
                            "sync", outT[eo * 128:(eo + 1) * 128,
                                         sc2 * 512:(sc2 + 1) * 512],
                            outstage[p], s_out[p])

            for sc2 in range(2):
                if sc2 == 0:
                    pend1 = list(pend1_init)
                else:
                    pend1 = pend1_next
                pend2_pre = []
                for ho in range(64):
                    b, wv = pend1[ho]
                    r = ho % 2
                    wait("tensor", s_w[b], wv)
                    if sc2 == 0 and ho < 2:
                        wait("tensor", s_dve, stt_dve[15])
                    if sc2 == 1 and ho < 2:
                        wait("tensor", s_act, gelu_act[(0, 63)])
                    if ho >= 2:
                        wait("tensor", s_act, gelu_act[(sc2, ho - 2)])
                    for e in range(16):
                        if sc2 == 0 and ho < 2:
                            wait("tensor", x1_done[(0, e)][0],
                                 x1_done[(0, e)][1])
                        mm(psA[:, r * 512:(r + 1) * 512],
                           wslab[b][:, e * 128:(e + 1) * 128],
                           pool_q[e][:, sc2 * 512:(sc2 + 1) * 512],
                           start=(e == 0), stop=(e == 15),
                           inc=(s_pe if e == 15 else None))
                    f1_pe[(sc2, ho)] = s_pe.n
                    wait("scalar", s_pe, f1_pe[(sc2, ho)])
                    if sc2 == 0:
                        wait("scalar", s_dve, pass3_done[ho // 4])
                    if sc2 == 1 and ho == 0:
                        wait("scalar", s_pe, f2_pe[(0, 15)])
                    gelu_act[(sc2, ho)] = act(
                        h_view(ho, sc2), psA[:, r * 512:(r + 1) * 512],
                        AF.Gelu, bias=b1_sb[:, ho:ho + 1], inc=s_act)
                    nho = ho + 6
                    if nho < 64:
                        pend1.append(load_slab(p1t[nho], f1_pe[(sc2, ho)]))
                    else:
                        qn = nho - 64
                        pend2_pre.append(load_slab(
                            p2t[qn // 4][:, (qn % 4) * 2048:
                                         (qn % 4 + 1) * 2048],
                            f1_pe[(sc2, ho)]))

                if sc2 == 1:
                    emit_ln2_out(0)
                for eo in range(2):
                    if sc2 == 0:
                        wait("sync", s_dve, stt_dve[15])
                    else:
                        wait("sync", s_dve, stt2_dve[(0, 15)])
                    wi = eo * 2 + sc2
                    wait("sync", s_x1w[wi % 2], x1w_dma[wi])
                    x1r_dma[(sc2, eo)] = dma(
                        "sync", x1r[eo],
                        x1_scr[eo * 128:(eo + 1) * 128,
                               sc2 * 512:(sc2 + 1) * 512], s_x1r[eo])
                def d_stats_mms(eo, sc2=sc2):
                    wait("tensor", s_dve, rbf2_dve[(sc2, eo)])
                    wait("tensor", s_act, sq2_act[(sc2, eo)])
                    if eo == 0 and sc2 == 1:
                        wait("tensor", s_dve, CP2_prev[0])
                    mm(psC[0:1, 0:512], ones_sq_sb[:, 0:1], rbf2,
                       start=(eo == 0), stop=(eo == 15))
                    mm(psC[0:1, 512:1024], ones_sq_sb[:, 0:1], rsq2,
                       start=(eo == 0), stop=(eo == 15), inc=s_pe)
                    stats2_pe[(sc2, eo)] = s_pe.n

                pend2 = pend2_pre
                qidx = 6
                pend1_next = []
                for eo in range(16):
                    r = eo % 2
                    if eo >= 2:
                        wait("tensor", s_dve, stt2_dve[(sc2, eo - 2)])
                    elif sc2 == 1:
                        wait("tensor", s_dve, stt2_dve[(0, 14 + eo)])
                    elif eo == 0:
                        wait("tensor", s_dve, CP1)
                    for q in range(4):
                        b, wv = pend2[eo * 4 + q]
                        wait("tensor", s_w[b], wv)
                        if eo == 0 and q == 0:
                            wait("tensor", s_act, gelu_act[(sc2, 63)])
                        for i in range(16):
                            hc = q * 16 + i
                            mm(psB2[:, r * 512:(r + 1) * 512],
                               wslab[b][:, i * 128:(i + 1) * 128],
                               h_view(hc, sc2),
                               start=(hc == 0), stop=(hc == 63),
                               inc=(s_pe if i == 15 else None))
                        qr_pe[(sc2, eo, q)] = s_pe.n
                        if qidx < 64:
                            neo, nq = divmod(qidx, 4)
                            peo, pq = divmod(qidx - 6, 4)
                            pend2.append(load_slab(
                                p2t[neo][:, nq * 2048:(nq + 1) * 2048],
                                qr_pe[(sc2, peo, pq)]))
                            qidx += 1
                        elif sc2 == 0 and qidx < 70:
                            peo, pq = divmod(qidx - 6, 4)
                            pend1_next.append(load_slab(
                                p1t[qidx - 64], qr_pe[(sc2, peo, pq)]))
                            qidx += 1
                    f2_pe[(sc2, eo)] = s_pe.n
                    if eo > 0:
                        d_stats_mms(eo - 1)
                    wait("vector", s_pe, f2_pe[(sc2, eo)])
                    wait("vector", s_x1r[r], x1r_dma[(sc2, eo)])
                    if sc2 == 0 and eo == 0:
                        wait("vector", s_pe, wo_pe[15])
                        if debug:
                            em("vector", lambda e, s=s_dbg: e.wait_ge(s.h, s.n))
                    stt2_dve[(sc2, eo)] = vec(
                        "scalar_tensor_tensor", r2_sb[eo],
                        psB2[:, r * 512:(r + 1) * 512],
                        b22_sb[:, eo:eo + 1], x1r[r],
                        ALU.add, ALU.add, inc=s_dve)
                    vdrain()
                    if eo > 0:
                        wait("vector", s_pe, stats2_pe[(sc2, eo - 1)])
                    rbf2_dve[(sc2, eo)] = vec("tensor_copy", rbf2, r2_sb[eo],
                                              inc=s_dve)
                    wait("scalar", s_dve, stt2_dve[(sc2, eo)])
                    if eo > 0:
                        wait("scalar", s_pe, stats2_pe[(sc2, eo - 1)])
                    sq2_act[(sc2, eo)] = act(rsq2, r2_sb[eo], AF.Square,
                                             inc=s_act)
                    neo = eo + 2
                    if neo < 16:
                        wait("sync", s_dve, stt2_dve[(sc2, eo)])
                        wi = neo * 2 + sc2
                        wait("sync", s_x1w[wi % 2], x1w_dma[wi])
                        x1r_dma[(sc2, neo)] = dma(
                            "sync", x1r[r],
                            x1_scr[neo * 128:(neo + 1) * 128,
                                   sc2 * 512:(sc2 + 1) * 512], s_x1r[r])

                d_stats_mms(15)
                emit_ln2_fin(sc2)
                if sc2 == 1:
                    emit_ln2_out(1)

            wait("sync", s_out[0], s_out[0].n)
            wait("sync", s_out[1], s_out[1].n)

        if debug:
            wait("sync", s_dbg, s_dbg.n)
            if "C" in phases or "D" in phases:
                wait("sync", s_x1w[0], s_x1w[0].n)
                wait("sync", s_x1w[1], s_x1w[1].n)
            wait("sync", s_st[0], s_st[0].n)
            wait("sync", s_st[1], s_st[1].n)
            for i in range(16):
                wait("sync", s_vt[i], 16)

        block = ctx.enter_context(nc.Block())

        @block.sync
        def _(e):
            for f in streams["sync"]:
                f(e)

        @block.gpsimd
        def _(e):
            for f in streams["gpsimd"]:
                f(e)

        @block.tensor
        def _(e):
            for f in streams["tensor"]:
                f(e)

        @block.scalar
        def _(e):
            for f in streams["scalar"]:
                f(e)

        @block.vector
        def _(e):
            for f in streams["vector"]:
                f(e)

    return nc


def stage_inputs(L0_attn_x, L0_attn_Wq, L0_attn_bq, L0_attn_Wk, L0_attn_bk,
                 L0_attn_Wv, L0_attn_bv, L0_attn_Wo, L0_attn_bo,
                 L0_attn_spiral_weight, L0_attn_helix_rotation,
                 L0_ffn_proj1, L0_ffn_proj1_bias, L0_ffn_proj2,
                 L0_ffn_proj2_bias, L0_ffn_gate_w):
    f = np.float32

    def slabs(W, oc, ec):
        A = np.asarray(W, f).T
        return np.ascontiguousarray(
            A.reshape(ec, 128, oc, 128).transpose(2, 1, 0, 3)
        ).reshape(oc, 128, ec * 128)

    shared = {
        "wqt": slabs(L0_attn_Wq, 16, 16),
        "wkt": slabs(L0_attn_Wk, 16, 16),
        "wvt": slabs(L0_attn_Wv, 16, 16),
        "wot": slabs(L0_attn_Wo, 16, 16),
        "p1t": slabs(L0_ffn_proj1, 64, 16),
        "p2t": slabs(L0_ffn_proj2, 16, 64),
        "bq2": np.ascontiguousarray(np.asarray(L0_attn_bq, f).reshape(16, 128).T),
        "bk2": np.ascontiguousarray(np.asarray(L0_attn_bk, f).reshape(16, 128).T),
        "bv2": np.ascontiguousarray(np.asarray(L0_attn_bv, f).reshape(16, 128).T),
        "bo2": np.ascontiguousarray(np.asarray(L0_attn_bo, f).reshape(16, 128).T),
        "b22": np.ascontiguousarray(
            np.asarray(L0_ffn_proj2_bias, f).reshape(16, 128).T),
        "b1b": np.ascontiguousarray(
            np.asarray(L0_ffn_proj1_bias, f).reshape(64, 128).T),
        "helix2": np.ascontiguousarray(
            np.asarray(L0_attn_helix_rotation, f).reshape(16, 128).T),
        "spiralr": np.asarray(L0_attn_spiral_weight, f).reshape(1, 16),
        "ones_sq": np.ones((128, 128), BF),
        "ones1f": np.ones((1, 128), f),
    }
    x = np.asarray(L0_attn_x, f)
    in_maps = []
    for c in range(8):
        b, half = divmod(c, 2)
        xtb = x[b].T
        own = xtb[:, half * 1024:(half + 1) * 1024]
        other = xtb[:, (1 - half) * 1024:(2 - half) * 1024]
        xt_c = np.ascontiguousarray(np.concatenate([own, other], axis=1))
        m = dict(shared)
        m["xt"] = xt_c
        in_maps.append(m)
    return in_maps


def kernel(**inputs):
    in_maps = stage_inputs(**inputs)
    nc = build()
    res = run_bass_kernel_spmd(nc, in_maps, core_ids=list(range(8)))
    out = np.empty((4, 2048, 2048), np.float32)
    for c in range(8):
        b, half = divmod(c, 2)
        out[b, half * 1024:(half + 1) * 1024, :] = res.results[c]["outT"].T
    return out


# version 26 (final)
# speedup vs baseline: 1.1759x; 1.1759x over previous
import contextlib
import math

import numpy as np
import ml_dtypes

import concourse.bass as bass
import concourse.mybir as mybir
from concourse.bass_utils import run_bass_kernel_spmd

F32 = mybir.dt.float32
BF16 = mybir.dt.bfloat16
AF = mybir.ActivationFunctionType
ALU = mybir.AluOpType
AX = mybir.AxisListType
BF = ml_dtypes.bfloat16

PI = math.pi
RSQRT_DH = 1.0 / math.sqrt(128.0)
EPS = 1e-5

ENGS = ("sync", "gpsimd", "tensor", "scalar", "vector")


class Sem:
    def __init__(self, ctx, nc, name):
        self.h = ctx.enter_context(nc.semaphore(name))
        self.n = 0
        self.name = name


def build(phases="ABCD", debug=False):
    nc = bass.Bass("TRN2", target_bir_lowering=False, debug=False)

    xt = nc.dram_tensor("xt", [2048, 2048], F32, kind="ExternalInput").ap()
    wqt = nc.dram_tensor("wqt", [16, 128, 2048], F32, kind="ExternalInput").ap()
    wkt = nc.dram_tensor("wkt", [16, 128, 2048], F32, kind="ExternalInput").ap()
    wvt = nc.dram_tensor("wvt", [16, 128, 2048], F32, kind="ExternalInput").ap()
    wot = nc.dram_tensor("wot", [16, 128, 2048], F32, kind="ExternalInput").ap()
    p1t = nc.dram_tensor("p1t", [64, 128, 2048], F32, kind="ExternalInput").ap()
    p2t = nc.dram_tensor("p2t", [16, 128, 8192], F32, kind="ExternalInput").ap()
    bq2 = nc.dram_tensor("bq2", [128, 16], F32, kind="ExternalInput").ap()
    bk2 = nc.dram_tensor("bk2", [128, 16], F32, kind="ExternalInput").ap()
    bv2 = nc.dram_tensor("bv2", [128, 16], F32, kind="ExternalInput").ap()
    bo2 = nc.dram_tensor("bo2", [128, 16], F32, kind="ExternalInput").ap()
    b22 = nc.dram_tensor("b22", [128, 16], F32, kind="ExternalInput").ap()
    b1b = nc.dram_tensor("b1b", [128, 64], F32, kind="ExternalInput").ap()
    helix2 = nc.dram_tensor("helix2", [128, 16], F32, kind="ExternalInput").ap()
    spiralr = nc.dram_tensor("spiralr", [1, 16], F32, kind="ExternalInput").ap()
    ones_sq = nc.dram_tensor("ones_sq", [128, 128], BF16, kind="ExternalInput").ap()
    ones1f = nc.dram_tensor("ones1f", [1, 128], F32, kind="ExternalInput").ap()

    def scratch(name, shape, dt):
        kind = "ExternalOutput" if debug else "Internal"
        return nc.dram_tensor(name, shape, dt, kind=kind).ap()

    k_scr = scratch("k_scr", [2048, 2048], BF16)
    v_scr = scratch("v_scr", [16, 2048, 128], BF16)
    x1_scr = scratch("x1_scr", [2048, 1024], F32)
    outT = nc.dram_tensor("outT", [2048, 1024], F32, kind="ExternalOutput").ap()
    if debug:
        dbg_q = nc.dram_tensor("dbg_q", [16, 128, 1024], BF16,
                               kind="ExternalOutput").ap()
        dbg_at = nc.dram_tensor("dbg_at", [16, 128, 1024], BF16,
                                kind="ExternalOutput").ap()
        dbg_nb = nc.dram_tensor("dbg_nb", [128, 1], F32,
                                kind="ExternalOutput").ap()

    with contextlib.ExitStack() as ctx:
        def sbuf(name, shape, dt, **kw):
            return ctx.enter_context(nc.sbuf_tensor(name, shape, dt, **kw))

        pool16 = [sbuf(f"pool16_{i}", [128, 2048], BF16, is_transpose_dest=True)
                  for i in range(16)]
        pool_q = [sbuf(f"poolq_{i}", [128, 1024], BF16) for i in range(16)]
        pool_a = [sbuf(f"poola_{i}", [128, 1024], BF16) for i in range(16)]
        wslab = [sbuf(f"wslab_{i}", [128, 2048], BF16) for i in range(6)]
        stage = [sbuf(f"stage_{i}", [128, 2048], BF16) for i in range(2)]
        stage2 = [sbuf(f"stage2_{i}", [128, 2048], BF16) for i in range(2)]
        pt_sb = [sbuf(f"pt_{i}", [128, 1024], BF16) for i in range(2)]
        tmp_f = sbuf("tmp_f", [128, 1024], F32)
        gp_tmp = sbuf("gp_tmp", [128, 1024], F32)
        mbc_sb = sbuf("mbc_sb", [128, 1024], F32)
        rbc_sb = sbuf("rbc_sb", [128, 1024], F32)
        rowsc = sbuf("rowsc", [1, 4096], F32)
        helix_sb = sbuf("helix_sb", [128, 16], F32)
        t_a = sbuf("t_a", [128, 16], F32)
        t_b = sbuf("t_b", [128, 16], F32)
        w_sb = sbuf("w_sb", [128, 16], F32)
        cw_sb = sbuf("cw_sb", [128, 16], F32)
        sinv = sbuf("sinv", [128, 16], F32)
        cosv = sbuf("cosv", [128, 16], F32)
        qsc = sbuf("qsc", [128, 16], F32)
        bqs = sbuf("bqs", [128, 16], F32)
        bks = sbuf("bks", [128, 16], F32)
        bq2_sb = sbuf("bq2_sb", [128, 16], F32)
        bk2_sb = sbuf("bk2_sb", [128, 16], F32)
        bv2_sb = sbuf("bv2_sb", [128, 16], F32)
        bo2_sb = sbuf("bo2_sb", [128, 16], F32)
        b22_sb = sbuf("b22_sb", [128, 16], F32)
        b1_sb = sbuf("b1_sb", [128, 64], F32)
        spiral_bc = sbuf("spiral_bc", [128, 16], F32)
        spiralr_sb = sbuf("spiralr_sb", [1, 16], F32)
        ones_sq_sb = sbuf("ones_sq_sb", [128, 128], BF16)
        ones1f_sb = sbuf("ones1f_sb", [1, 128], F32)
        qkn = sbuf("qkn", [1, 128], F32)
        negB = sbuf("negB", [128, 1], F32)

        psA = ctx.enter_context(nc.psum_tensor("psA", [128, 2048], F32))
        psB2 = ctx.enter_context(nc.psum_tensor("psB2", [128, 1024], F32))
        psC = ctx.enter_context(nc.psum_tensor("psC", [128, 1024], F32))

        r_sb = [pool16[i][:].bitcast(F32) for i in range(16)]
        r2_sb = [pool_a[i][:].bitcast(F32) for i in range(16)]
        x1stage = [stage2[i][:].bitcast(F32) for i in range(2)]
        recip = stage2[0][:].bitcast(F32)
        denraw_f = stage2[0][:].bitcast(F32)
        avraw_f = stage2[1][:].bitcast(F32)
        xres_v = [stage[0][:].bitcast(F32), stage[1][:].bitcast(F32)]
        x1r = [stage[0][:].bitcast(F32)[:, 0:512],
               stage[0][:].bitcast(F32)[:, 512:1024]]
        outstage = [stage[1][:].bitcast(F32)[:, 0:512],
                    stage[1][:].bitcast(F32)[:, 512:1024]]
        rbf = pt_sb[0]
        rsq = pt_sb[1]
        rbf2 = pt_sb[0][:, 0:512]
        rsq2 = pt_sb[1][:, 0:512]

        def S(name):
            return Sem(ctx, nc, name)

        s_bias = S("s_bias")
        s_xt_t = [S(f"s_xt{i}") for i in range(16)]
        s_w = [S(f"s_w{i}") for i in range(6)]
        s_st = [S("s_stA"), S("s_stB")]
        s_vt = [S(f"s_vt{i}") for i in range(16)]
        s_kh = [S("s_khA"), S("s_khB")]
        s_xr = [S("s_xrA"), S("s_xrB")]
        s_x1w = [S("s_x1wA"), S("s_x1wB")]
        s_x1r = [S("s_x1rA"), S("s_x1rB")]
        s_out = [S("s_outA"), S("s_outB")]
        s_pe = S("s_pe")
        s_gp = S("s_gp")
        s_act = S("s_act")
        s_dve = S("s_dve")
        s_dbg = S("s_dbg")

        streams = {k: [] for k in ENGS}
        last_wait = {k: {} for k in ENGS}
        defer_state = {"on": False, "bufs": None}

        def em(eng, fn):
            if defer_state["on"]:
                defer_state["bufs"][eng].append(fn)
            else:
                streams[eng].append(fn)

        def begin_defer():
            defer_state["on"] = True
            defer_state["bufs"] = {k: [] for k in ENGS}

        def end_defer():
            defer_state["on"] = False
            bufs = defer_state["bufs"]
            defer_state["bufs"] = None
            return bufs

        def splice(bufs):
            for k in ENGS:
                streams[k].extend(bufs[k])

        def wait(eng, sem, val):
            if val is None or val <= 0:
                return
            if defer_state["on"]:
                em(eng, lambda e, s=sem.h, v=val: e.wait_ge(s, v))
                return
            lw = last_wait[eng]
            if lw.get(sem.name, -1) >= val:
                return
            lw[sem.name] = val
            em(eng, lambda e, s=sem.h, v=val: e.wait_ge(s, v))

        def dma(eng, out_ap, in_ap, sem, **kw):
            sem.n += 16
            def f(e, o=out_ap, i=in_ap, s=sem.h, kw=kw):
                e.dma_start(out=o, in_=i, **kw).then_inc(s, 16)
            em(eng, f)
            return sem.n

        def dma_t(eng, out_ap, in_ap, sem):
            sem.n += 16
            def f(e, o=out_ap, i=in_ap, s=sem.h):
                e.dma_start_transpose(o, i).then_inc(s, 16)
            em(eng, f)
            return sem.n

        def mm(out_ap, l_ap, r_ap, start, stop, inc=None):
            if inc is not None:
                inc.n += 1
            def f(t, o=out_ap, l=l_ap, r=r_ap, st=start, sp=stop,
                  s=(inc.h if inc else None)):
                ins = t.matmul(o, l, r, start=st, stop=sp)
                if s is not None:
                    ins.then_inc(s, 1)
            em("tensor", f)
            return inc.n if inc else None

        def act(out_ap, in_ap, func, scale=1.0, bias=0.0, accum=None, inc=None):
            if inc is not None:
                inc.n += 1
            def f(a, o=out_ap, i=in_ap, fu=func, sc=scale, bi=bias,
                  au=accum, s=(inc.h if inc else None)):
                ins = a.activation(out=o, in_=i, func=fu, scale=sc, bias=bi,
                                   accum_out=au)
                if s is not None:
                    ins.then_inc(s, 1)
            em("scalar", f)
            return inc.n if inc else None

        def vec(name, *args, inc=None, **kw):
            if inc is not None:
                inc.n += 1
            def f(v, n=name, a=args, kw=kw, s=(inc.h if inc else None)):
                ins = getattr(v, n)(*a, **kw)
                if s is not None:
                    ins.then_inc(s, 1)
            em("vector", f)
            return inc.n if inc else None

        def vdrain():
            em("vector", lambda v: v.drain())

        def gvec(name, *args, inc=None, **kw):
            if inc is not None:
                inc.n += 1
            def f(g, n=name, a=args, kw=kw, s=(inc.h if inc else None)):
                ins = getattr(g, n)(*a, **kw)
                if s is not None:
                    ins.then_inc(s, 1)
            em("gpsimd", f)
            return inc.n if inc else None

        def adrain():
            em("scalar", lambda a: a.drain())

        wload_cnt = [0] * 6

        def load_slab(src_ap, gate_pe):
            b = sum(wload_cnt) % 6
            if gate_pe:
                wait("gpsimd", s_pe, gate_pe)
            dma("gpsimd", wslab[b][:], src_ap, s_w[b], max_dma_last_dim=8192)
            wload_cnt[b] += 1
            return b, 16 * wload_cnt[b]

        for dst, src in ((helix_sb, helix2), (bq2_sb, bq2), (bk2_sb, bk2),
                         (bv2_sb, bv2), (bo2_sb, bo2), (b22_sb, b22),
                         (b1_sb, b1b), (spiralr_sb, spiralr),
                         (ones_sq_sb, ones_sq), (ones1f_sb, ones1f)):
            dma("sync", dst[:], src[:], s_bias)
        BIAS_ALL = s_bias.n

        for e in range(16):
            dma("gpsimd", pool16[e][:], xt[e * 128:(e + 1) * 128, :],
                s_xt_t[e], max_dma_last_dim=8192)

        wait("tensor", s_bias, BIAS_ALL)
        m_spiral = mm(psC[:, 0:16], ones1f_sb[:], spiralr_sb[:], True, True,
                      inc=s_pe)
        wait("vector", s_bias, BIAS_ALL)
        wait("vector", s_pe, m_spiral)
        vec("tensor_copy", spiral_bc[:], psC[:, 0:16])
        vdrain()
        vec("tensor_single_scalar", t_a[:], helix_sb[:], PI, ALU.is_gt)
        vdrain()
        vec("scalar_tensor_tensor", t_b[:], t_a[:], -2 * PI, helix_sb[:],
            ALU.mult, ALU.add)
        vdrain()
        vec("tensor_single_scalar", t_a[:], t_b[:], -PI, ALU.is_lt)
        vdrain()
        vec("scalar_tensor_tensor", w_sb[:], t_a[:], 2 * PI, t_b[:],
            ALU.mult, ALU.add)
        vdrain()
        vec("tensor_single_scalar", t_b[:], w_sb[:], PI / 2, ALU.add)
        vdrain()
        vec("tensor_single_scalar", t_a[:], t_b[:], PI, ALU.is_gt)
        vdrain()
        m_trig = vec("scalar_tensor_tensor", cw_sb[:], t_a[:], -2 * PI, t_b[:],
                     ALU.mult, ALU.add, inc=s_dve)
        wait("scalar", s_dve, m_trig)
        act(sinv[:], w_sb[:], AF.Sin)
        adrain()
        m_sin = act(cosv[:], cw_sb[:], AF.Sin, inc=s_act)
        wait("vector", s_act, m_sin)
        vec("scalar_tensor_tensor", qsc[:], cosv[:], RSQRT_DH, spiral_bc[:],
            ALU.mult, ALU.mult)
        vdrain()
        vec("tensor_mul", bqs[:], bq2_sb[:], qsc[:])
        vdrain()
        vec("memset", qkn[0:1, 120:121], EPS)
        vec("memset", qkn[0:1, 122:123], -0.5)
        vdrain()
        CONST_DONE = vec("tensor_mul", bks[:], bk2_sb[:], sinv[:], inc=s_dve)

        j_pe = {}
        q_evac_act = {}
        qn_dve = {}
        k_evac_act = {}
        kst_dma = {}
        kn_dve = {}
        v_evac_act = {}
        vst_dma = {}

        def proj_phase(tag, w_srcs, consumer, gates, init_gates):
            nw = min(6, len(w_srcs))
            pend = [load_slab(w_srcs[i], init_gates[i]) for i in range(nw)]
            for j in range(len(w_srcs)):
                b, wv = pend[j]
                wait("tensor", s_w[b], wv)
                consumer(j, b)
                nj = j + 6
                if nj < len(w_srcs):
                    pend.append(load_slab(w_srcs[nj], gates(j)))

        q_sq_act = {}

        def q_norm_mms(j):
            wait("tensor", s_act, q_sq_act[j])
            if j > 0:
                wait("tensor", s_dve, qn_dve[j - 1])
            for sc in range(2):
                mm(psC[0:1, sc * 512:(sc + 1) * 512], ones_sq_sb[:, 0:1],
                   pt_sb[0][:, sc * 512:(sc + 1) * 512], True, True,
                   inc=(s_pe if sc == 1 else None))
            j_pe[("qn", j)] = s_pe.n
            wait("vector", s_pe, j_pe[("qn", j)])
            qn_dve[j] = vec("tensor_reduce", qkn[0:1, j:j + 1],
                            psC[0:1, 0:1024], AX.X, ALU.max, inc=s_dve)

        def q_consumer(j, b):
            rq = j % 2
            if j >= 2:
                wait("tensor", s_act, q_evac_act[j - 2])
            for e in range(16):
                if j == 0:
                    wait("tensor", s_xt_t[e], 16)
                for sc in range(2):
                    mm(psA[:, rq * 1024 + sc * 512:rq * 1024 + (sc + 1) * 512],
                       wslab[b][:, e * 128:(e + 1) * 128],
                       pool16[e][:, sc * 512:(sc + 1) * 512],
                       start=(e == 0), stop=(e == 15),
                       inc=(s_pe if (e == 15 and sc == 1) else None))
            j_pe[("q", j)] = s_pe.n
            if j > 0:
                q_norm_mms(j - 1)
            wait("scalar", s_dve, CONST_DONE)
            wait("scalar", s_pe, j_pe[("q", j)])
            q_evac_act[j] = act(pool_q[j][:],
                                psA[:, rq * 1024:(rq + 1) * 1024],
                                AF.Identity, scale=qsc[:, j:j + 1],
                                bias=bqs[:, j:j + 1], inc=s_act)
            adrain()
            if j > 0:
                wait("scalar", s_pe, j_pe[("qn", j - 1)])
            q_sq_act[j] = act(pt_sb[0][:], pool_q[j][:], AF.Square, inc=s_act)

        proj_phase("q", [wqt[j] for j in range(16)], q_consumer,
                   lambda j: j_pe[("q", j)], [None] * 6)
        q_norm_mms(15)

        k_sq_act = {}
        kn_prev = [None]

        def k_norm_mms(jh):
            j, half = jh
            wait("tensor", s_act, k_sq_act[(j, half)])
            prev = (j, 0) if half == 1 else ((j - 1, 1) if j > 0 else None)
            if prev is not None:
                wait("tensor", s_dve, kn_dve[prev])
            else:
                wait("tensor", s_dve, qn_dve[15])
            for sc in range(2):
                mm(psC[0:1, sc * 512:(sc + 1) * 512], ones_sq_sb[:, 0:1],
                   stage2[0][:, half * 1024 + sc * 512:
                             half * 1024 + (sc + 1) * 512],
                   True, True, inc=(s_pe if sc == 1 else None))
            j_pe[("kn", j, half)] = s_pe.n
            wait("vector", s_pe, j_pe[("kn", j, half)])
            kn_dve[(j, half)] = vec(
                "tensor_reduce",
                qkn[0:1, 16 + 32 * half + j:17 + 32 * half + j],
                psC[0:1, 0:1024], AX.X, ALU.max, inc=s_dve)

        def k_consumer(j, b):
            p = j % 2
            for half in range(2):
                if j > 0:
                    wait("tensor", s_act, k_evac_act[(j - 1, half)])
                else:
                    wait("tensor", s_act, q_evac_act[14 + half])
                for e in range(16):
                    for sc in range(2):
                        mm(psA[:, half * 1024 + sc * 512:
                               half * 1024 + (sc + 1) * 512],
                           wslab[b][:, e * 128:(e + 1) * 128],
                           pool16[e][:, half * 1024 + sc * 512:
                                     half * 1024 + (sc + 1) * 512],
                           start=(e == 0), stop=(e == 15),
                           inc=(s_pe if (e == 15 and sc == 1) else None))
                j_pe[("k", j, half)] = s_pe.n
                if kn_prev[0] is not None:
                    k_norm_mms(kn_prev[0])
                wait("scalar", s_pe, j_pe[("k", j, half)])
                if half == 0 and j >= 2:
                    wait("scalar", s_st[p], kst_dma[j - 2])
                k_evac_act[(j, half)] = act(
                    stage[p][:, half * 1024:(half + 1) * 1024],
                    psA[:, half * 1024:(half + 1) * 1024], AF.Identity,
                    scale=sinv[:, j:j + 1], bias=bks[:, j:j + 1], inc=s_act)
                adrain()
                if kn_prev[0] is not None:
                    wait("scalar", s_pe, j_pe[("kn",) + kn_prev[0]])
                k_sq_act[(j, half)] = act(
                    stage2[0][:, half * 1024:(half + 1) * 1024],
                    stage[p][:, half * 1024:(half + 1) * 1024],
                    AF.Square, inc=s_act)
                kn_prev[0] = (j, half)
            wait("sync", s_act, k_evac_act[(j, 1)])
            kst_dma[j] = dma("sync", k_scr[j * 128:(j + 1) * 128, :],
                             stage[p][:], s_st[p])

        proj_phase("k", [wkt[j] for j in range(16)], k_consumer,
                   lambda j: j_pe[("k", j, 1)],
                   [j_pe[("q", 10 + i)] for i in range(6)])
        k_norm_mms(kn_prev[0])

        vdrain()
        vec("tensor_max", qkn[0:1, 80:96], qkn[0:1, 16:32], qkn[0:1, 48:64])
        vdrain()
        vec("tensor_mul", qkn[0:1, 96:112], qkn[0:1, 0:16], qkn[0:1, 80:96])
        vdrain()
        mB = vec("tensor_reduce", qkn[0:1, 112:113], qkn[0:1, 96:112],
                 AX.X, ALU.max, inc=s_dve)
        wait("scalar", s_dve, mB)
        mBs = act(qkn[0:1, 113:114], qkn[0:1, 112:113], AF.Sqrt, inc=s_act)
        wait("vector", s_act, mBs)
        mBn = vec("tensor_single_scalar", qkn[0:1, 114:115],
                  qkn[0:1, 113:114], -1.0, ALU.mult, inc=s_dve)
        wait("tensor", s_dve, mBn)
        mBb = mm(psC[:, 0:1], ones1f_sb[:], qkn[0:1, 114:115], True, True,
                 inc=s_pe)
        wait("scalar", s_pe, mBb)
        NEGB_ACT = act(negB[:], psC[:, 0:1], AF.Copy, inc=s_act)

        def v_consumer(j, b):
            p = j % 2
            for half in range(2):
                if j > 0:
                    wait("tensor", s_act, v_evac_act[(j - 1, half)])
                else:
                    wait("tensor", s_act, k_evac_act[(15, half)])
                if j == 0 and half == 0:
                    wait("tensor", s_act, NEGB_ACT)
                for e in range(16):
                    for sc in range(2):
                        mm(psA[:, half * 1024 + sc * 512:
                               half * 1024 + (sc + 1) * 512],
                           wslab[b][:, e * 128:(e + 1) * 128],
                           pool16[e][:, half * 1024 + sc * 512:
                                     half * 1024 + (sc + 1) * 512],
                           start=(e == 0), stop=(e == 15),
                           inc=(s_pe if (e == 15 and sc == 1) else None))
                j_pe[("v", j, half)] = s_pe.n
                wait("scalar", s_pe, j_pe[("v", j, half)])
                if half == 0:
                    if j >= 2:
                        wait("scalar", s_st[p], vst_dma[j - 2])
                    else:
                        wait("scalar", s_st[p], kst_dma[14 + j])
                v_evac_act[(j, half)] = act(
                    stage[p][:, half * 1024:(half + 1) * 1024],
                    psA[:, half * 1024:(half + 1) * 1024], AF.Identity,
                    bias=bv2_sb[:, j:j + 1], inc=s_act)
            wait("sync", s_act, v_evac_act[(j, 1)])
            vst_dma[j] = dma("sync",
                             v_scr[:, j * 128:(j + 1) * 128, :]
                             .rearrange("tc p c -> p tc c"),
                             stage[p][:].rearrange("p (tc c) -> p tc c",
                                                   c=128),
                             s_st[p])

        wait("sync", s_st[0], kst_dma[0])
        kh0_dma = dma("sync", stage2[1][:], k_scr[0:128, :], s_kh[0])
        proj_phase("v", [wvt[j] for j in range(16)], v_consumer,
                   lambda j: j_pe[("v", j, 1)],
                   [j_pe[("k", 10 + i, 1)] for i in range(6)])
        V_LAST_PE = j_pe[("v", 15, 1)]

        wait("sync", s_st[0], s_st[0].n)
        wait("sync", s_st[1], s_st[1].n)
        wait("sync", s_pe, V_LAST_PE)
        dma_t("sync", pool16[0][:], v_scr[0], s_vt[0])
        dma_t("sync", pool16[1][:], v_scr[1], s_vt[1])
        kh1_dma = dma("sync", stage[1][:], k_scr[128:256, :], s_kh[1])
        for tc in range(2, 16):
            dma_t("sync", pool16[tc][:], v_scr[tc], s_vt[tc])
        wait("scalar", s_st[0], s_st[0].n)
        wait("scalar", s_st[1], s_st[1].n)
        wait("scalar", s_pe, V_LAST_PE)
        act(qkn[0:1, 121:122], qkn[0:1, 120:121], AF.Exp)

        if debug:
            for j in range(16):
                wait("sync", s_act, q_evac_act[j])
                dma("sync", dbg_q[j], pool_q[j][:], s_dbg)
            wait("sync", s_act, NEGB_ACT)
            dma("sync", dbg_nb[:], negB[:], s_dbg)
        DBG_A = s_dbg.n

        if "B" in phases or "C" in phases or "D" in phases:
            kh_dma = {}
            exp_act = {}
            avcp_act = {}
            avcp_dve = {}
            avd_pe = {}
            sc_pe = {}
            norm_dve = {}
            kh_dma[0] = kh0_dma
            kh_dma[1] = kh1_dma
            def emit_davd(h, tc):
                pt = pt_buf(h, tc)
                wait("tensor", s_act, exp_act[(h, tc)])
                if h == 0:
                    wait("tensor", s_vt[tc], 16)
                if tc == 0 and h > 0:
                    wait("tensor", s_act, avcp_act[h - 1])
                    wait("tensor", s_dve, avcp_dve[h - 1])
                for sc in range(2):
                    mm(psC[:, sc * 512:(sc + 1) * 512], ones_sq_sb[:],
                       pt[:, sc * 512:(sc + 1) * 512],
                       start=(tc == 0), stop=(tc == 15))
                for sc in range(2):
                    mm(psB2[:, sc * 512:(sc + 1) * 512],
                       pool16[tc][:, h * 128:(h + 1) * 128],
                       pt[:, sc * 512:(sc + 1) * 512],
                       start=(tc == 0), stop=(tc == 15),
                       inc=(s_pe if sc == 1 else None))
                avd_pe[(h, tc)] = s_pe.n

            def pt_buf(h, tc):
                if h == 0:
                    return [pt_sb[0], pt_sb[1], pool_a[14], pool_a[15]][tc % 4]
                return pt_sb[tc % 2]

            for h in range(16):
                p = h % 2
                kh_buf = stage2[1] if h == 0 else stage[p]
                wait("tensor", s_kh[p], kh_dma[h])
                for tc in range(16):
                    r = tc % 2
                    if tc >= 2:
                        wait("tensor", s_act, exp_act[(h, tc - 2)])
                    elif h > 0:
                        wait("tensor", s_act, exp_act[(h - 1, 14 + tc)])
                    else:
                        wait("tensor", s_act, v_evac_act[(15, tc)])
                    for sc in range(2):
                        mm(psA[:, r * 1024 + sc * 512:
                               r * 1024 + (sc + 1) * 512],
                           kh_buf[:, tc * 128:(tc + 1) * 128],
                           pool_q[h][:, sc * 512:(sc + 1) * 512],
                           True, True, inc=(s_pe if sc == 1 else None))
                    sc_pe[(h, tc)] = s_pe.n
                    wait("scalar", s_act, NEGB_ACT)
                    wait("scalar", s_pe, sc_pe[(h, tc)])
                    if h == 0:
                        if tc >= 4:
                            wait("scalar", s_pe, avd_pe[(h, tc - 4)])
                    elif tc >= 2:
                        wait("scalar", s_pe, avd_pe[(h, tc - 2)])
                    else:
                        wait("scalar", s_pe, avd_pe[(h - 1, 14 + tc)])
                    exp_act[(h, tc)] = act(
                        pt_buf(h, tc)[:], psA[:, r * 1024:(r + 1) * 1024],
                        AF.Exp, bias=negB[:], inc=s_act)
                    if tc > 0:
                        emit_davd(h, tc - 1)
                emit_davd(h, 15)
                wait("scalar", s_pe, avd_pe[(h, 15)])
                if h > 0:
                    wait("scalar", s_dve, norm_dve[h - 1])
                avcp_act[h] = act(denraw_f[:, 0:1024], psC[:], AF.Copy,
                                  inc=s_act)
                wait("vector", s_pe, avd_pe[(h, 15)])
                avcp_dve[h] = vec("tensor_copy", avraw_f[:, 0:1024], psB2[:],
                                  inc=s_dve)
                vdrain()
                wait("vector", s_act, avcp_act[h])
                vec("reciprocal", tmp_f[:], denraw_f[:, 0:1024])
                vdrain()
                norm_dve[h] = vec("tensor_mul", pool_a[h][:],
                                  avraw_f[:, 0:1024], tmp_f[:], inc=s_dve)
                vdrain()
                nh = h + 2
                if nh < 16:
                    pn = nh % 2
                    wait("sync", s_pe, sc_pe[(h, 15)])
                    kh_dma[nh] = dma("sync", stage[pn][:],
                                     k_scr[nh * 128:(nh + 1) * 128, :],
                                     s_kh[pn])
            ATT_DONE_DVE = norm_dve[15]
            ATT_LAST_EXP = exp_act[(15, 15)]
            ATT_LAST_AVD = avd_pe[(15, 15)]
            ATT_LAST_SC = sc_pe[(15, 15)]

            if debug:
                for h in range(16):
                    wait("sync", s_dve, norm_dve[h])
                    dma("sync", dbg_at[h], pool_a[h][:], s_dbg)
            DBG_B = s_dbg.n

        if "C" in phases or "D" in phases:
            wo_pe = {}
            stt_dve = {}
            rbf_dve = {}
            sq_act = {}
            stats_pe = {}
            xr_dma = {}
            x1_dve = {}
            x1w_dma = {}

            wait("sync", s_pe, sc_pe[(14, 15)])
            xr_dma[0] = dma("sync", xres_v[0], xt[0:128, 0:1024], s_xr[0])
            wait("sync", s_pe, ATT_LAST_SC)
            xr_dma[1] = dma("sync", xres_v[1], xt[128:256, 0:1024], s_xr[1])

            def c_stats_mms(j):
                wait("tensor", s_dve, rbf_dve[j])
                wait("tensor", s_act, sq_act[j])
                for sc in range(2):
                    mm(psB2[0:1, sc * 512:(sc + 1) * 512], ones_sq_sb[:, 0:1],
                       rbf[:, sc * 512:(sc + 1) * 512],
                       start=(j == 0), stop=(j == 15))
                for sc in range(2):
                    mm(psC[0:1, sc * 512:(sc + 1) * 512], ones_sq_sb[:, 0:1],
                       rsq[:, sc * 512:(sc + 1) * 512],
                       start=(j == 0), stop=(j == 15),
                       inc=(s_pe if sc == 1 else None))
                stats_pe[j] = s_pe.n

            pend = [load_slab(wot[i], V_LAST_PE) for i in range(6)]
            for j in range(16):
                b, wv = pend[j]
                wait("tensor", s_w[b], wv)
                r = j % 2
                if j < 2:
                    wait("tensor", s_act, ATT_LAST_EXP)
                    wait("tensor", s_dve, ATT_DONE_DVE)
                else:
                    wait("tensor", s_dve, stt_dve[j - 2])
                for e in range(16):
                    for sc in range(2):
                        mm(psA[:, r * 1024 + sc * 512:
                               r * 1024 + (sc + 1) * 512],
                           wslab[b][:, e * 128:(e + 1) * 128],
                           pool_a[e][:, sc * 512:(sc + 1) * 512],
                           start=(e == 0), stop=(e == 15),
                           inc=(s_pe if (e == 15 and sc == 1) else None))
                wo_pe[j] = s_pe.n
                if j > 0:
                    c_stats_mms(j - 1)
                wait("vector", s_pe, wo_pe[j])
                wait("vector", s_xr[r], xr_dma[j])
                if j == 0:
                    wait("vector", s_pe, ATT_LAST_AVD)
                stt_dve[j] = vec("scalar_tensor_tensor", r_sb[j],
                                 psA[:, r * 1024:(r + 1) * 1024],
                                 bo2_sb[:, j:j + 1], xres_v[r],
                                 ALU.add, ALU.add, inc=s_dve)
                vdrain()
                if j > 0:
                    wait("vector", s_pe, stats_pe[j - 1])
                rbf_dve[j] = vec("tensor_copy", rbf[:], r_sb[j], inc=s_dve)
                wait("scalar", s_dve, stt_dve[j])
                if j > 0:
                    wait("scalar", s_pe, stats_pe[j - 1])
                sq_act[j] = act(rsq[:], r_sb[j], AF.Square, inc=s_act)
                nj = j + 2
                if nj < 16:
                    wait("sync", s_dve, stt_dve[j])
                    xr_dma[nj] = dma("sync", xres_v[r],
                                     xt[nj * 128:(nj + 1) * 128, 0:1024],
                                     s_xr[r])
                if j + 6 < 16:
                    pend.append(load_slab(wot[j + 6], wo_pe[j]))
            c_stats_mms(15)

            wait("vector", s_pe, stats_pe[15])
            vec("tensor_single_scalar", rowsc[0:1, 0:1024], psB2[0:1, 0:1024],
                1.0 / 2048.0, ALU.mult)
            vec("tensor_single_scalar", rowsc[0:1, 1024:2048],
                psC[0:1, 0:1024], 1.0 / 2048.0, ALU.mult)
            vdrain()
            vec("tensor_mul", rowsc[0:1, 2048:3072], rowsc[0:1, 0:1024],
                rowsc[0:1, 0:1024])
            vdrain()
            m_var = vec("tensor_sub", rowsc[0:1, 3072:4096],
                        rowsc[0:1, 1024:2048], rowsc[0:1, 2048:3072],
                        inc=s_dve)
            wait("scalar", s_dve, m_var)
            act(rowsc[0:1, 2048:3072], rowsc[0:1, 3072:4096], AF.Ln,
                bias=qkn[0:1, 120:121])
            adrain()
            m_rstd = act(rowsc[0:1, 1024:2048], rowsc[0:1, 2048:3072],
                         AF.Exp, scale=qkn[0:1, 122:123], inc=s_act)
            wait("tensor", s_dve, m_var)
            for sc in range(2):
                mm(psB2[:, sc * 512:(sc + 1) * 512], ones1f_sb[:],
                   rowsc[0:1, sc * 512:(sc + 1) * 512], True, True,
                   inc=(s_pe if sc == 1 else None))
            MB1_PE = s_pe.n
            wait("vector", s_pe, MB1_PE)
            vec("tensor_copy", mbc_sb[:], psB2[:])
            wait("tensor", s_act, m_rstd)
            for sc in range(2):
                mm(psC[:, sc * 512:(sc + 1) * 512], ones1f_sb[:],
                   rowsc[0:1, 1024 + sc * 512:1024 + (sc + 1) * 512],
                   True, True, inc=(s_pe if sc == 1 else None))
            BC1_PE = s_pe.n

            wait("vector", s_pe, BC1_PE)
            CP1 = vec("tensor_copy", rbc_sb[:], psC[:], inc=s_dve)
            vdrain()
            wait("vector", s_pe, ATT_LAST_SC)
            if debug:
                em("vector", lambda e, s=s_dbg: e.wait_ge(s.h, s.n))
            wait("gpsimd", s_dve, CP1)
            wait("gpsimd", s_pe, ATT_LAST_SC)
            if debug:
                em("gpsimd", lambda e, s=s_dbg: e.wait_ge(s.h, s.n))
            pend1_init = [load_slab(p1t[i], wo_pe[10 + i]) for i in range(6)]
            x1_done = {}
            pass3_done = {}
            h5_0 = slice(0, 512)
            for j0 in range(0, 16, 2):
                if j0 < 8:
                    eng, es, vf, tw = "vector", s_dve, vec, tmp_f
                else:
                    eng, es, vf, tw = "gpsimd", s_gp, gvec, gp_tmp
                for k in range(2):
                    vf("tensor_sub", tw[:, k * 512:(k + 1) * 512],
                       r_sb[j0 + k][:, h5_0], mbc_sb[:, h5_0], inc=es)
                if eng == "vector":
                    vdrain()
                else:
                    wait(eng, es, es.n)
                for k in range(2):
                    x1_done[(0, j0 + k)] = (es, vf(
                        "tensor_mul", pool_q[j0 + k][:, h5_0],
                        tw[:, k * 512:(k + 1) * 512], rbc_sb[:, h5_0],
                        inc=es))
                if eng == "vector":
                    vdrain()
                else:
                    wait(eng, es, es.n)
            X1_H0 = [x1_done[(0, 7)], x1_done[(0, 15)]]

            idx = 0
            for j in range(16):
                h5_1 = slice(512, 1024)
                vec("tensor_sub", tmp_f[:, 0:512], r_sb[j][:, h5_1],
                    mbc_sb[:, h5_1])
                vdrain()
                x1_done[(1, j)] = (s_dve, vec(
                    "tensor_mul", pool_q[j][:, h5_1], tmp_f[:, 0:512],
                    rbc_sb[:, h5_1], inc=s_dve))
                vdrain()
                for half in range(2):
                    h5 = slice(half * 512, (half + 1) * 512)
                    p = idx % 2
                    x1sl = x1stage[p][:, 0:512]
                    if idx >= 2:
                        wait("vector", s_x1w[p], x1w_dma[idx - 2])
                    vec("tensor_sub", tmp_f[:, 0:512], r_sb[j][:, h5],
                        mbc_sb[:, h5])
                    vdrain()
                    mv = vec("tensor_mul", x1sl, tmp_f[:, 0:512],
                             rbc_sb[:, h5], inc=s_dve)
                    vdrain()
                    wait("sync", s_dve, mv)
                    x1w_dma[idx] = dma(
                        "sync", x1_scr[j * 128:(j + 1) * 128, h5],
                        x1sl, s_x1w[p])
                    idx += 1
                pass3_done[j] = mv
            X1_ALL = [x1_done[(0, 15)], (s_dve, pass3_done[15])]

        if "D" in phases:
            f1_pe = {}
            gelu_act = {}
            qr_pe = {}
            f2_pe = {}
            stt2_dve = {}
            rbf2_dve = {}
            sq2_act = {}
            stats2_pe = {}
            x1r_dma = {}
            out_done = {}
            out_dma = {}
            CP2_prev = [0]
            ln2_deferred = [None]

            def h_view(hc, sc2):
                return pool16[hc // 4][:, (hc % 4) * 512:(hc % 4 + 1) * 512]

            ln2_rstd = {}

            def emit_ln2_fin(sc2):
                wait("vector", s_pe, stats2_pe[(sc2, 15)])
                vec("tensor_single_scalar", rowsc[0:1, 0:512],
                    psC[0:1, 0:512], 1.0 / 2048.0, ALU.mult)
                vec("tensor_single_scalar", rowsc[0:1, 512:1024],
                    psC[0:1, 512:1024], 1.0 / 2048.0, ALU.mult)
                vdrain()
                vec("tensor_mul", rowsc[0:1, 1024:1536], rowsc[0:1, 0:512],
                    rowsc[0:1, 0:512])
                vdrain()
                m_var2 = vec("tensor_sub", rowsc[0:1, 1536:2048],
                             rowsc[0:1, 512:1024], rowsc[0:1, 1024:1536],
                             inc=s_dve)
                wait("scalar", s_dve, m_var2)
                act(rowsc[0:1, 1024:1536], rowsc[0:1, 1536:2048], AF.Ln,
                    bias=qkn[0:1, 120:121])
                adrain()
                m_rstd2 = act(rowsc[0:1, 512:1024], rowsc[0:1, 1024:1536],
                              AF.Exp, scale=qkn[0:1, 122:123], inc=s_act)
                ln2_rstd[sc2] = m_rstd2
                wait("tensor", s_act, m_rstd2)
                mm(psC[:, 0:512], ones1f_sb[:], rowsc[0:1, 0:512], True, True)
                mm(psC[:, 512:1024], ones1f_sb[:], rowsc[0:1, 512:1024],
                   True, True, inc=s_pe)
                BC2_PE = s_pe.n
                wait("vector", s_pe, BC2_PE)
                if sc2 == 0:
                    wait("vector", s_gp, x1_done[(0, 15)][1])
                else:
                    wait("vector", out_done[(0, 14)][0],
                         out_done[(0, 14)][1])
                vec("tensor_copy", mbc_sb[:, 0:512], psC[:, 0:512])
                CP2 = vec("tensor_copy", rbc_sb[:, 0:512], psC[:, 512:1024],
                          inc=s_dve)
                vdrain()
                CP2_prev[0] = CP2
                if sc2 == 1:
                    wait("gpsimd", s_dve, CP2)

            def emit_ln2_out(sc2):

                    for eo in range(16):
                        p = eo % 2
                        if eo % 2 == 0 and sc2 == 1:
                            eng, es, ew = "gpsimd", s_gp, gp_tmp[:, 0:512]
                            vf = gvec
                        else:
                            eng, es, ew = "vector", s_dve, tmp_f[:, 0:512]
                            vf = vec
                        wait(eng, s_dve, CP2_prev[0])
                        if eo >= 2:
                            wait(eng, s_out[p], out_dma[(sc2, eo - 2)])
                        elif sc2 == 1:
                            wait(eng, s_out[p], out_dma[(0, 14 + eo)])
                        vf("tensor_sub", ew, r2_sb[eo], mbc_sb[:, 0:512], inc=es)
                        if eng == "vector":
                            vdrain()
                        else:
                            wait(eng, es, es.n)
                        out_done[(sc2, eo)] = (es, vf(
                            "tensor_mul", outstage[p], ew, rbc_sb[:, 0:512],
                            inc=es))
                        if eng == "vector":
                            vdrain()
                        wait("sync", es, out_done[(sc2, eo)][1])
                        out_dma[(sc2, eo)] = dma(
                            "sync", outT[eo * 128:(eo + 1) * 128,
                                         sc2 * 512:(sc2 + 1) * 512],
                            outstage[p], s_out[p])

            for sc2 in range(2):
                if sc2 == 0:
                    pend1 = list(pend1_init)
                else:
                    pend1 = pend1_next
                pend2_pre = []
                for ho in range(64):
                    b, wv = pend1[ho]
                    r = ho % 2
                    wait("tensor", s_w[b], wv)
                    if sc2 == 0 and ho < 2:
                        wait("tensor", s_dve, stt_dve[15])
                    if sc2 == 1 and ho < 2:
                        wait("tensor", s_act, gelu_act[(0, 63)])
                    if ho >= 2:
                        wait("tensor", s_act, gelu_act[(sc2, ho - 2)])
                    for e in range(16):
                        if sc2 == 0 and ho < 2:
                            wait("tensor", x1_done[(0, e)][0],
                                 x1_done[(0, e)][1])
                        mm(psA[:, r * 512:(r + 1) * 512],
                           wslab[b][:, e * 128:(e + 1) * 128],
                           pool_q[e][:, sc2 * 512:(sc2 + 1) * 512],
                           start=(e == 0), stop=(e == 15),
                           inc=(s_pe if e == 15 else None))
                    f1_pe[(sc2, ho)] = s_pe.n
                    wait("scalar", s_pe, f1_pe[(sc2, ho)])
                    if sc2 == 0:
                        wait("scalar", s_dve, pass3_done[ho // 4])
                    if sc2 == 1 and ho == 0:
                        wait("scalar", s_pe, f2_pe[(0, 15)])
                    gelu_act[(sc2, ho)] = act(
                        h_view(ho, sc2), psA[:, r * 512:(r + 1) * 512],
                        AF.Gelu, bias=b1_sb[:, ho:ho + 1], inc=s_act)
                    nho = ho + 6
                    if nho < 64:
                        pend1.append(load_slab(p1t[nho], f1_pe[(sc2, ho)]))
                    else:
                        qn = nho - 64
                        pend2_pre.append(load_slab(
                            p2t[qn // 4][:, (qn % 4) * 2048:
                                         (qn % 4 + 1) * 2048],
                            f1_pe[(sc2, ho)]))

                if sc2 == 1:
                    emit_ln2_out(0)
                for eo in range(2):
                    if sc2 == 0:
                        wait("sync", s_dve, stt_dve[15])
                    else:
                        wait("sync", s_dve, stt2_dve[(0, 15)])
                    wi = eo * 2 + sc2
                    wait("sync", s_x1w[wi % 2], x1w_dma[wi])
                    x1r_dma[(sc2, eo)] = dma(
                        "sync", x1r[eo],
                        x1_scr[eo * 128:(eo + 1) * 128,
                               sc2 * 512:(sc2 + 1) * 512], s_x1r[eo])
                def d_stats_mms(eo, sc2=sc2):
                    wait("tensor", s_dve, rbf2_dve[(sc2, eo)])
                    wait("tensor", s_act, sq2_act[(sc2, eo)])
                    if eo == 0 and sc2 == 1:
                        wait("tensor", s_dve, CP2_prev[0])
                    mm(psC[0:1, 0:512], ones_sq_sb[:, 0:1], rbf2,
                       start=(eo == 0), stop=(eo == 15))
                    mm(psC[0:1, 512:1024], ones_sq_sb[:, 0:1], rsq2,
                       start=(eo == 0), stop=(eo == 15), inc=s_pe)
                    stats2_pe[(sc2, eo)] = s_pe.n

                pend2 = pend2_pre
                qidx = 6
                pend1_next = []
                for eo in range(16):
                    r = eo % 2
                    if eo >= 2:
                        wait("tensor", s_dve, stt2_dve[(sc2, eo - 2)])
                    elif sc2 == 1:
                        wait("tensor", s_dve, stt2_dve[(0, 14 + eo)])
                    elif eo == 0:
                        wait("tensor", s_dve, CP1)
                    for q in range(4):
                        b, wv = pend2[eo * 4 + q]
                        wait("tensor", s_w[b], wv)
                        if eo == 0 and q == 0:
                            wait("tensor", s_act, gelu_act[(sc2, 63)])
                        for i in range(16):
                            hc = q * 16 + i
                            mm(psB2[:, r * 512:(r + 1) * 512],
                               wslab[b][:, i * 128:(i + 1) * 128],
                               h_view(hc, sc2),
                               start=(hc == 0), stop=(hc == 63),
                               inc=(s_pe if i == 15 else None))
                        qr_pe[(sc2, eo, q)] = s_pe.n
                        if qidx < 64:
                            neo, nq = divmod(qidx, 4)
                            peo, pq = divmod(qidx - 6, 4)
                            pend2.append(load_slab(
                                p2t[neo][:, nq * 2048:(nq + 1) * 2048],
                                qr_pe[(sc2, peo, pq)]))
                            qidx += 1
                        elif sc2 == 0 and qidx < 70:
                            peo, pq = divmod(qidx - 6, 4)
                            pend1_next.append(load_slab(
                                p1t[qidx - 64], qr_pe[(sc2, peo, pq)]))
                            qidx += 1
                    f2_pe[(sc2, eo)] = s_pe.n
                    if eo > 0:
                        d_stats_mms(eo - 1)
                    wait("vector", s_pe, f2_pe[(sc2, eo)])
                    wait("vector", s_x1r[r], x1r_dma[(sc2, eo)])
                    if sc2 == 0 and eo == 0:
                        wait("vector", s_pe, wo_pe[15])
                        if debug:
                            em("vector", lambda e, s=s_dbg: e.wait_ge(s.h, s.n))
                    stt2_dve[(sc2, eo)] = vec(
                        "scalar_tensor_tensor", r2_sb[eo],
                        psB2[:, r * 512:(r + 1) * 512],
                        b22_sb[:, eo:eo + 1], x1r[r],
                        ALU.add, ALU.add, inc=s_dve)
                    vdrain()
                    if eo > 0:
                        wait("vector", s_pe, stats2_pe[(sc2, eo - 1)])
                    rbf2_dve[(sc2, eo)] = vec("tensor_copy", rbf2, r2_sb[eo],
                                              inc=s_dve)
                    wait("scalar", s_dve, stt2_dve[(sc2, eo)])
                    if eo > 0:
                        wait("scalar", s_pe, stats2_pe[(sc2, eo - 1)])
                    sq2_act[(sc2, eo)] = act(rsq2, r2_sb[eo], AF.Square,
                                             inc=s_act)
                    neo = eo + 2
                    if neo < 16:
                        wait("sync", s_dve, stt2_dve[(sc2, eo)])
                        wi = neo * 2 + sc2
                        wait("sync", s_x1w[wi % 2], x1w_dma[wi])
                        x1r_dma[(sc2, neo)] = dma(
                            "sync", x1r[r],
                            x1_scr[neo * 128:(neo + 1) * 128,
                                   sc2 * 512:(sc2 + 1) * 512], s_x1r[r])

                d_stats_mms(15)
                emit_ln2_fin(sc2)
                if sc2 == 1:
                    emit_ln2_out(1)

            wait("sync", s_out[0], s_out[0].n)
            wait("sync", s_out[1], s_out[1].n)

        if debug:
            wait("sync", s_dbg, s_dbg.n)
            if "C" in phases or "D" in phases:
                wait("sync", s_x1w[0], s_x1w[0].n)
                wait("sync", s_x1w[1], s_x1w[1].n)
            wait("sync", s_st[0], s_st[0].n)
            wait("sync", s_st[1], s_st[1].n)
            for i in range(16):
                wait("sync", s_vt[i], 16)

        block = ctx.enter_context(nc.Block())

        @block.sync
        def _(e):
            for f in streams["sync"]:
                f(e)

        @block.gpsimd
        def _(e):
            for f in streams["gpsimd"]:
                f(e)

        @block.tensor
        def _(e):
            for f in streams["tensor"]:
                f(e)

        @block.scalar
        def _(e):
            for f in streams["scalar"]:
                f(e)

        @block.vector
        def _(e):
            for f in streams["vector"]:
                f(e)

    return nc


def stage_inputs(L0_attn_x, L0_attn_Wq, L0_attn_bq, L0_attn_Wk, L0_attn_bk,
                 L0_attn_Wv, L0_attn_bv, L0_attn_Wo, L0_attn_bo,
                 L0_attn_spiral_weight, L0_attn_helix_rotation,
                 L0_ffn_proj1, L0_ffn_proj1_bias, L0_ffn_proj2,
                 L0_ffn_proj2_bias, L0_ffn_gate_w):
    f = np.float32

    def slabs(W, oc, ec):
        A = np.asarray(W, f).T
        return np.ascontiguousarray(
            A.reshape(ec, 128, oc, 128).transpose(2, 1, 0, 3)
        ).reshape(oc, 128, ec * 128)

    shared = {
        "wqt": slabs(L0_attn_Wq, 16, 16),
        "wkt": slabs(L0_attn_Wk, 16, 16),
        "wvt": slabs(L0_attn_Wv, 16, 16),
        "wot": slabs(L0_attn_Wo, 16, 16),
        "p1t": slabs(L0_ffn_proj1, 64, 16),
        "p2t": slabs(L0_ffn_proj2, 16, 64),
        "bq2": np.ascontiguousarray(np.asarray(L0_attn_bq, f).reshape(16, 128).T),
        "bk2": np.ascontiguousarray(np.asarray(L0_attn_bk, f).reshape(16, 128).T),
        "bv2": np.ascontiguousarray(np.asarray(L0_attn_bv, f).reshape(16, 128).T),
        "bo2": np.ascontiguousarray(np.asarray(L0_attn_bo, f).reshape(16, 128).T),
        "b22": np.ascontiguousarray(
            np.asarray(L0_ffn_proj2_bias, f).reshape(16, 128).T),
        "b1b": np.ascontiguousarray(
            np.asarray(L0_ffn_proj1_bias, f).reshape(64, 128).T),
        "helix2": np.ascontiguousarray(
            np.asarray(L0_attn_helix_rotation, f).reshape(16, 128).T),
        "spiralr": np.asarray(L0_attn_spiral_weight, f).reshape(1, 16),
        "ones_sq": np.ones((128, 128), BF),
        "ones1f": np.ones((1, 128), f),
    }
    x = np.asarray(L0_attn_x, f)
    in_maps = []
    for c in range(8):
        b, half = divmod(c, 2)
        xtb = x[b].T
        own = xtb[:, half * 1024:(half + 1) * 1024]
        other = xtb[:, (1 - half) * 1024:(2 - half) * 1024]
        xt_c = np.ascontiguousarray(np.concatenate([own, other], axis=1))
        m = dict(shared)
        m["xt"] = xt_c
        in_maps.append(m)
    return in_maps


def kernel(**inputs):
    in_maps = stage_inputs(**inputs)
    nc = build()
    res = run_bass_kernel_spmd(nc, in_maps, core_ids=list(range(8)))
    out = np.empty((4, 2048, 2048), np.float32)
    for c in range(8):
        b, half = divmod(c, 2)
        out[b, half * 1024:(half + 1) * 1024, :] = res.results[c]["outT"].T
    return out
